# version 22
# baseline (speedup 1.0000x reference)
import numpy as np
import ml_dtypes
import concourse.bass as bass
import concourse.mybir as mybir
from concourse.bass_utils import run_bass_kernel_spmd
from contextlib import ExitStack

F32 = mybir.dt.float32
BF16 = mybir.dt.bfloat16
I32 = mybir.dt.int32
AF = mybir.ActivationFunctionType
ALU = mybir.AluOpType
AX = mybir.AxisListType

D = 1024
NCH = 8
S = 2048
TC = 512
NTC = S // TC
EPS = 1e-5
QS = 1.0 / 127.0

SUB = {"l0a": True, "l0f": True, "l1a": True, "l1f": True}
DBG = False
NSEQ = 2


class Sem:
    def __init__(self, nc, stack, name):
        self.h = stack.enter_context(nc.semaphore(name))
        self.n = 0

    def inc(self, ins, k=1):
        ins.then_inc(self.h, k)
        self.n += k
        return ins


class G:
    pass


def build():
    nc = bass.Bass("TRN2", target_bir_lowering=False)
    g = G()
    g.nc = nc

    x_in = nc.dram_tensor("x_in", [NSEQ, D, S], F32, kind="ExternalInput").ap()
    x_out = nc.dram_tensor("x_out", [NSEQ, D, S], F32, kind="ExternalOutput").ap()
    W = {}
    for name, shape, dt in [
        ("l0a_qkv", [3 * D, D], F32), ("l0a_proj", [D, D], F32), ("l0a_bias", [D], F32),
        ("l0f_wr", [D, 16], F32),
        ("l0f_wg", [16, D, 512], F32), ("l0f_wu", [16, D, 512], F32), ("l0f_wd", [16, 512, D], F32),
        ("l0f_sg", [D, D], F32), ("l0f_su", [D, D], F32), ("l0f_sd", [D, D], F32),
        ("l1a_wq", [16, D, 64], F32), ("l1a_wk", [16, D, 64], F32), ("l1a_wv", [16, D, 64], F32),
        ("l1a_bq", [16, 64], F32), ("l1a_bk", [16, 64], F32), ("l1a_bv", [16, 64], F32),
        ("l1f_wg", [3072, D], I32), ("l1f_wu", [3072, D], I32), ("l1f_wd", [D, 3072], I32),
    ]:
        W[name] = nc.dram_tensor(name, shape, dt, kind="ExternalInput").ap()
    g.W = W
    c_ones = nc.dram_tensor("c_ones", [128, 128], BF16, kind="ExternalInput").ap()
    c_id = nc.dram_tensor("c_id", [128, 128], BF16, kind="ExternalInput").ap()
    c_mask = nc.dram_tensor("c_mask", [128, 128], F32, kind="ExternalInput").ap()
    c_maskb = nc.dram_tensor("c_maskb", [128, 128], BF16, kind="ExternalInput").ap()
    c_sel = nc.dram_tensor("c_sel", [4, 512], BF16, kind="ExternalInput").ap()
    g.dbg_bf = nc.dram_tensor("dbg_bf", [128, 16384], BF16, kind="ExternalOutput").ap()
    g.dbg_f = nc.dram_tensor("dbg_f", [128, 4096], F32, kind="ExternalOutput").ap()
    g.dbg_used = [0, 0]

    DW = {}
    DW["qkvT"] = nc.dram_tensor("d_qkvT", [24, 128, NCH, 128], BF16).ap()
    DW["projT"] = nc.dram_tensor("d_projT", [8, 128, NCH, 128], BF16).ap()
    DW["wgT"] = nc.dram_tensor("d_wgT", [24, 128, NCH, 128], BF16).ap()
    DW["wuT"] = nc.dram_tensor("d_wuT", [24, 128, NCH, 128], BF16).ap()
    DW["wdT"] = nc.dram_tensor("d_wdT", [8, 128, 24, 128], BF16).ap()
    DW["ewg"] = nc.dram_tensor("d_ewg", [16, 128, NCH, 512], BF16).ap()
    DW["ewu"] = nc.dram_tensor("d_ewu", [16, 128, NCH, 512], BF16).ap()
    DW["ewd"] = nc.dram_tensor("d_ewd", [16, 128, 4, 1024], BF16).ap()
    DW["sg"] = nc.dram_tensor("d_sg", [128, NCH, 1024], BF16).ap()
    DW["su"] = nc.dram_tensor("d_su", [128, NCH, 1024], BF16).ap()
    DW["sd"] = nc.dram_tensor("d_sd", [128, NCH, 1024], BF16).ap()
    DW["wr"] = nc.dram_tensor("d_wr", [128, NCH, 16], BF16).ap()
    DW["aq"] = nc.dram_tensor("d_aq", [8, 128, NCH, 128], BF16).ap()
    DW["ak"] = nc.dram_tensor("d_ak", [8, 128, NCH, 128], BF16).ap()
    DW["av"] = nc.dram_tensor("d_av", [8, 128, NCH, 128], BF16).ap()
    DW["den"] = nc.dram_tensor("d_den", [32, S], BF16).ap()
    DW["cw"] = nc.dram_tensor("d_cw", [16, S], BF16).ap()
    g.DW = DW

    with ExitStack() as stack:
        def sb(name, shape, dtype):
            return stack.enter_context(nc.sbuf_tensor(name, shape, dtype))

        g.x = sb("x_res", [128, NCH, S], F32)
        g.scr = sb("scratch", [128, 14336], BF16)
        g.biga = sb("biga", [128, 16384], BF16)
        g.bigb = sb("bigb", [128, 16384], BF16)
        g.accb = sb("accb", [128, NCH, TC], F32)
        g.smf = sb("sm_f32", [128, 4096], F32)
        g.smb = sb("sm_bf", [128, 5120], BF16)
        g.ones = sb("ones", [128, 128], BF16)
        g.ident = sb("ident", [128, 128], BF16)
        g.cmask = sb("cmask", [128, 128], F32)
        g.cmaskb = sb("cmaskb", [128, 128], BF16)
        g.sel = sb("sel", [4, 512], BF16)
        g.bias_cols = sb("bias_cols", [128, 32], F32)
        g.bias_colsq = sb("bias_colsq", [128, 8], F32)

        g.Rb = g.smb[:, 0:512]
        g.MRb = g.smb[:, 512:1024]
        g.sg_t = [g.smb[:, 1024:1536], g.smb[:, 1536:2048]]
        g.rq = g.smb[:, 2048:2560]
        g.cwq = g.smb[:, 2560:3072]
        g.rt = g.smb[:, 3072:3328]
        g.lnrb = g.smb[:, 3328:4352]
        g.smb_spare = g.smb[:, 4352:5120]
        g.ctxf = g.smf[:, 2048:3072]

        ps = lambda n, shp: stack.enter_context(nc.psum_tensor(n, shp, F32))
        g.psA = ps("ps_A", [128, 2048])
        g.psB = ps("ps_B", [128, 512])
        g.psC = ps("ps_C", [128, 512])
        g.ps_bc = ps("ps_bc", [128, 1024])
        g.pp = [g.psA[:, 512 * i:512 * (i + 1)] for i in range(4)] + [g.psB[:], g.psC[:]]

        g.sW = Sem(nc, stack, "sW")
        g.sX = Sem(nc, stack, "sX")
        g.sO = Sem(nc, stack, "sO")
        g.sPE = Sem(nc, stack, "sPE")
        g.sDV = Sem(nc, stack, "sDV")
        g.sAC = Sem(nc, stack, "sAC")
        g.sPL = Sem(nc, stack, "sPL")
        g.sR = [Sem(nc, stack, f"sR{i}") for i in range(16)]

        g.ps_bc_rd_dv = 0
        g.ps_bc_rd_misc = []
        g.lnsq_rd_pe = 0
        g.rb_rd = []
        g.pp_wr_gate = [[] for _ in range(6)]

        block = stack.enter_context(nc.Block())

        @block.sync
        def _(sync):
            emit_all(g, x_in, x_out, c_ones, c_id, c_mask, c_maskb, c_sel)

    return nc


def pe(g): return g.nc.tensor
def dv(g): return g.nc.vector
def ac(g): return g.nc.scalar
def pl(g): return g.nc.gpsimd
def sy(g): return g.nc.sync


def dump_bf(g, ap, n, rows=128):
    if not DBG:
        return
    off = g.dbg_used[0]
    g.dbg_used[0] += n
    sy(g).wait_ge(g.sDV.h, g.sDV.n)
    sy(g).wait_ge(g.sPL.h, g.sPL.n)
    sy(g).wait_ge(g.sAC.h, g.sAC.n)
    g.sO.inc(sy(g).dma_start(out=g.dbg_bf[0:rows, off:off + n], in_=ap), 16)


def dump_f(g, ap, n):
    if not DBG:
        return
    off = g.dbg_used[1]
    g.dbg_used[1] += n
    sy(g).wait_ge(g.sDV.h, g.sDV.n)
    sy(g).wait_ge(g.sPL.h, g.sPL.n)
    sy(g).wait_ge(g.sAC.h, g.sAC.n)
    g.sO.inc(sy(g).dma_start(out=g.dbg_f[:, off:off + n], in_=ap), 16)


def barrier(g):
    sems = [g.sW, g.sX, g.sO, g.sPE, g.sDV, g.sAC, g.sPL] + g.sR
    for eng in (pe(g), dv(g), ac(g), pl(g), sy(g)):
        for s in sems:
            if s.n:
                eng.wait_ge(s.h, s.n)


def emit_all(g, x_in, x_out, c_ones, c_id, c_mask, c_maskb, c_sel):
    g.sX.inc(pl(g).dma_start(out=g.ones[:], in_=c_ones), 16)
    g.sX.inc(pl(g).dma_start(out=g.ident[:], in_=c_id), 16)
    g.sX.inc(pl(g).dma_start(out=g.cmask[:], in_=c_mask), 16)
    g.sX.inc(pl(g).dma_start(out=g.cmaskb[:], in_=c_maskb), 16)
    g.sX.inc(pl(g).dma_start(out=g.sel[:], in_=c_sel), 16)
    for c in range(NCH):
        g.sX.inc(pl(g).dma_start(out=g.x[:, c, :], in_=x_in[0, 128 * c:128 * (c + 1), :]), 16)

    emit_prep(g)

    xst = 0
    for s in range(NSEQ):
        if s > 0:
            pl(g).wait_ge(g.sO.h, xst)
            for c in range(NCH):
                g.sX.inc(pl(g).dma_start(out=g.x[:, c, :], in_=x_in[s, 128 * c:128 * (c + 1), :]), 16)
        dv(g).wait_ge(g.sX.h, g.sX.n)
        ac(g).wait_ge(g.sX.h, g.sX.n)

        if SUB["l0a"]:
            emit_l0_attn(g, s)
            barrier(g)
        if SUB["l0f"]:
            emit_l0_ffn(g, s)
            barrier(g)
        if SUB["l1a"]:
            emit_l1_attn(g, s)
            barrier(g)
        if SUB["l1f"]:
            emit_l1_ffn(g, s)
            barrier(g)

        for c in range(NCH):
            g.sO.inc(sy(g).dma_start(out=x_out[s, 128 * c:128 * (c + 1), :], in_=g.x[:, c, :]), 16)
        xst = g.sO.n
    sy(g).wait_ge(g.sO.h, g.sO.n)


def emit_prep(g):
    nc = g.nc
    W, DW = g.W, g.DW
    stage_in = g.scr[:, 0:3072]
    stage_out = g.scr[:, 3072:6144]
    jobs = [
        (W["l0a_qkv"], DW["qkvT"], 24, NCH),
        (W["l0a_proj"], DW["projT"], 8, NCH),
        (W["l1f_wg"], DW["wgT"], 24, NCH),
        (W["l1f_wu"], DW["wuT"], 24, NCH),
        (W["l1f_wd"], DW["wdT"], 8, 24),
    ]
    pe(g).wait_ge(g.sX.h, 48)
    g.prep_job_o = {}
    so_stage = None
    ps_bf = g.ps_bc[:].bitcast(BF16)
    for src, dst, n_ocb, n_k in jobs:
        kin = n_k * 128
        for ocb in range(n_ocb):
            pl(g).wait_ge(g.sPE.h, g.sPE.n)
            ins = pl(g).dma_start(out=stage_in[:, 0:kin], in_=src[128 * ocb:128 * (ocb + 1), :])
            g.sX.inc(ins, 16)
            pe_ = pe(g)
            pe_.wait_ge(g.sX.h, g.sX.n)
            pe_.wait_ge(g.sDV.h, g.sDV.n)
            for k in range(n_k):
                kk = k % 16
                ins = pe_.transpose(ps_bf[:, 128 * kk:128 * (kk + 1)],
                                    stage_in[:, 128 * k:128 * (k + 1)], g.ident[:])
                if kk == 15 or k == n_k - 1:
                    g.sPE.inc(ins)
                    dv_ = dv(g)
                    dv_.wait_ge(g.sPE.h, g.sPE.n)
                    if so_stage is not None:
                        dv_.wait_ge(so_stage[0].h, so_stage[1])
                    lo = 128 * (k - kk)
                    ins2 = dv_.tensor_copy(stage_out[:, lo:128 * (k + 1)], ps_bf[:, 0:128 * (kk + 1)])
                    g.sDV.inc(ins2)
                    if k != n_k - 1:
                        pe_.wait_ge(g.sDV.h, g.sDV.n)
            sy(g).wait_ge(g.sDV.h, g.sDV.n)
            ins = sy(g).dma_start(out=dst[ocb].rearrange("p k e -> p (k e)"), in_=stage_out[:, 0:kin])
            jsem = g.sR[13] if dst.name in ("d_qkvT", "d_projT") else g.sR[14]
            jsem.inc(ins, 16)
            so_stage = (jsem, jsem.n)
            g.prep_job_o[dst.name] = (jsem, jsem.n)

    g.prep_tr13 = g.sR[13].n
    g.prep_tr14 = g.sR[14].n
    pl_ = sy(g)
    g.sX.inc(pl_.dma_start(out=g.smf[0:1, 0:1024], in_=W["l0a_bias"].unsqueeze(0)), 16)
    g.sX.inc(pl_.dma_start(out=g.smf[0:1, 1024:2048], in_=W["l1a_bq"].rearrange("h e -> (h e)").unsqueeze(0)), 16)
    g.sX.inc(pl_.dma_start(out=g.smf[0:1, 2048:3072], in_=W["l1a_bk"].rearrange("h e -> (h e)").unsqueeze(0)), 16)
    g.sX.inc(pl_.dma_start(out=g.smf[0:1, 3072:4096], in_=W["l1a_bv"].rearrange("h e -> (h e)").unsqueeze(0)), 16)
    g.sDV.inc(dv(g).memset(g.accb[0:1, 0, 0:1], 1.0))
    one_f32 = g.accb[0:1, 0, 0:1]
    pe_ = pe(g)
    pe_.wait_ge(g.sX.h, g.sX.n)
    pe_.wait_ge(g.sDV.h, g.sDV.n)
    for c in range(32):
        ins = pe_.matmul(g.ps_bc[:, c:c + 1], g.smf[0:1, 128 * c:128 * (c + 1)], one_f32,
                         start=True, stop=True)
    g.sPE.inc(ins)
    dv_ = dv(g)
    dv_.wait_ge(g.sPE.h, g.sPE.n)
    g.sDV.inc(dv_.tensor_copy(g.bias_cols[:], g.ps_bc[:, 0:32]))
    dv_.wait_ge(g.sDV.h, g.sDV.n)
    g.sDV.inc(dv_.tensor_scalar(g.bias_colsq[:], g.bias_cols[:, 8:16], 0.125, None, ALU.mult))
    g.prep_done_dv = g.sDV.n
    g.prep_all_o = 0
    cj = []
    for e in range(16):
        cj.append((W["l0f_wg"][e].rearrange("(k p) o -> p k o", p=128), DW["ewg"][e]))
        cj.append((W["l0f_wu"][e].rearrange("(k p) o -> p k o", p=128), DW["ewu"][e]))
        cj.append((W["l0f_wd"][e].rearrange("(k p) o -> p k o", p=128), DW["ewd"][e]))
    for nm, dnm in (("l0f_sg", "sg"), ("l0f_su", "su"), ("l0f_sd", "sd")):
        r = W[nm].rearrange("(k p) o -> p k o", p=128)
        cj.append((r[:, 0:4, :], DW[dnm][:, 0:4, :]))
        cj.append((r[:, 4:8, :], DW[dnm][:, 4:8, :]))
    cj.append((W["l0f_wr"].rearrange("(k p) o -> p k o", p=128), DW["wr"]))
    for gp in range(8):
        for nm, dst in (("l1a_wq", "aq"), ("l1a_wk", "ak"), ("l1a_wv", "av")):
            for j in range(2):
                cj.append((W[nm][2 * gp + j].rearrange("(k p) e -> p k e", p=128),
                           DW[dst][gp].rearrange("p k (h e) -> p k h e", h=2)[:, :, j, :]))
    cstage = [g.scr[:, 6144 + 4096 * i: 6144 + 4096 * (i + 1)] for i in range(2)]

    cmark = [0, 0]
    for i, (src, dst) in enumerate(cj):
        n = int(np.prod(src.shape[1:]))
        slot = cstage[i % 2][:, 0:n]
        pl(g).wait_ge(g.sO.h, cmark[i % 2])
        ins = pl(g).dma_start(out=slot, in_=src)
        g.sX.inc(ins, 16)
        pl(g).wait_ge(g.sX.h, g.sX.n)
        ins = pl(g).dma_start(out=dst, in_=slot)
        g.sO.inc(ins, 16)
        cmark[i % 2] = g.sO.n
    g.prep_all_o = g.sO.n

    g.ps_bc_rd_dv = g.sDV.n



def emit_ln(g, in_f32, out_bf, L, tmp_bf=None, residual=None):
    dst = out_bf if residual is None else tmp_bf
    dv_, ac_, pe_ = dv(g), ac(g), pe(g)

    for c in range(NCH):
        ins = dv_.tensor_copy(dst[c], in_f32[c])
    g.sDV.inc(ins)
    castm = g.sDV.n
    ac_.wait_ge(g.sDV.h, castm)
    ac_.wait_ge(g.sPE.h, g.lnsq_rd_pe)
    for c in range(NCH):
        ins = ac_.activation(g.ln_sq[c], dst[c], AF.Square)
    g.sAC.inc(ins)
    sqm = g.sAC.n
    pe_.wait_ge(g.sDV.h, max(castm, g.ps_bc_rd_dv))
    for sm, mk in g.ps_bc_rd_misc:
        pe_.wait_ge(sm.h, mk)
    g.ps_bc_rd_misc = []
    for c in range(NCH):
        pe_.matmul(g.ps_bc[0:1, 0:L], g.ones[:, 0:1], dst[c], start=(c == 0), stop=(c == NCH - 1))
    pe_.wait_ge(g.sAC.h, sqm)
    for c in range(NCH):
        ins = pe_.matmul(g.ps_bc[0:1, 512:512 + L], g.ones[:, 0:1], g.ln_sq[c],
                         start=(c == 0), stop=(c == NCH - 1))
    g.sPE.inc(ins)
    g.lnsq_rd_pe = g.sPE.n
    R = g.smf
    dv_.wait_ge(g.sPE.h, g.sPE.n)
    dv_.tensor_scalar(R[0:1, 0:L], g.ps_bc[0:1, 0:L], 1.0 / D, None, ALU.mult)
    ins = dv_.tensor_scalar(R[0:1, 512:512 + L], g.ps_bc[0:1, 512:512 + L], 1.0 / D, EPS, ALU.mult, ALU.add)
    g.sDV.inc(ins)
    g.ps_bc_rd_dv = g.sDV.n
    dv_.wait_ge(g.sDV.h, g.sDV.n)
    g.sDV.inc(dv_.tensor_tensor(R[0:1, 1024:1024 + L], R[0:1, 0:L], R[0:1, 0:L], ALU.mult))
    dv_.wait_ge(g.sDV.h, g.sDV.n)
    g.sDV.inc(dv_.tensor_tensor(R[0:1, 1536:1536 + L], R[0:1, 512:512 + L], R[0:1, 1024:1024 + L], ALU.subtract))
    ac_.wait_ge(g.sDV.h, g.sDV.n)
    g.sAC.inc(ac_.activation(R[0:1, 1024:1024 + L], R[0:1, 1536:1536 + L], AF.Ln))
    ac_.wait_ge(g.sAC.h, g.sAC.n)
    g.sAC.inc(ac_.activation(g.lnrb[0:1, 0:L], R[0:1, 1024:1024 + L], AF.Exp, scale=-0.5))
    dv_.wait_ge(g.sAC.h, g.sAC.n)
    g.sDV.inc(dv_.tensor_tensor(g.lnrb[0:1, 512:512 + L], R[0:1, 0:L], g.lnrb[0:1, 0:L], ALU.mult))
    pe_.wait_ge(g.sDV.h, g.sDV.n)
    pe_.matmul(g.ps_bc[:, 0:L], g.ones[0:1, :], g.lnrb[0:1, 512:512 + L], start=True, stop=True)
    ins = pe_.matmul(g.ps_bc[:, 512:512 + L], g.ones[0:1, :], g.lnrb[0:1, 0:L], start=True, stop=True)
    g.sPE.inc(ins)
    dv_.wait_ge(g.sPE.h, g.sPE.n)
    for sm, mk in g.rb_rd:
        dv_.wait_ge(sm.h, mk)
    dv_.tensor_copy(g.Rb[:, 0:L], g.ps_bc[:, 512:512 + L])
    g.sDV.inc(dv_.tensor_copy(g.MRb[:, 0:L], g.ps_bc[:, 0:L]))
    g.ps_bc_rd_dv = g.sDV.n
    dv_.wait_ge(g.sDV.h, g.sDV.n)
    for c in range(NCH):
        ins = dv_.tensor_tensor(dst[c], dst[c], g.Rb[:, 0:L], ALU.mult)
    g.sDV.inc(ins)
    dv_.wait_ge(g.sDV.h, g.sDV.n)
    for c in range(NCH):
        ins = dv_.tensor_tensor(dst[c], dst[c], g.MRb[:, 0:L], ALU.subtract)
    g.sDV.inc(ins)
    g.rb_rd = [(g.sDV, g.sDV.n)]
    if residual is not None:
        dv_.wait_ge(g.sDV.h, g.sDV.n)
        for c in range(NCH):
            ins = dv_.tensor_tensor(residual[c], residual[c], dst[c], ALU.add)
        g.sDV.inc(ins)
    return g.sDV.n


def emit_l1_ffn(g, s):
    nc = g.nc
    scr = g.scr
    xn = [scr[:, 0:4096].rearrange("p (c t) -> p c t", c=NCH)[:, c, :] for c in range(NCH)]
    gu_slot = [scr[:, 4096 + 1024 * i: 4096 + 1024 * (i + 1)] for i in range(4)]
    wd_slot = [scr[:, 12288 + 1024 * 0: 12288 + 1024 * 0 + 0]]
    wd_slot = [scr[:, 8192 + 3072 * i: 8192 + 3072 * (i + 1)] for i in range(2)]
    h = g.bigb
    g.ln_sq = [g.biga[:, 512 * c:512 * (c + 1)] for c in range(NCH)]
    tmp_bf = [g.biga[:, 4096 + 512 * c: 4096 + 512 * (c + 1)] for c in range(NCH)]

    slot_mark = [0] * 4
    wdm = [0, 0]
    gate_bank_rd = [0] * 4
    up_bank_rd = [0] * 4
    down_bank_rd = [0, 0]
    sg_rd = [0, 0]
    acc_rd = 0
    xn_rd = 0
    h_rd = 0

    sy(g).wait_ge(g.sR[14].h, g.prep_tr14)
    ac(g).wait_ge(g.sR[14].h, g.prep_tr14)

    for tq in range(NTC):
        tsl = slice(TC * tq, TC * (tq + 1))
        xin = [g.x[:, c, tsl] for c in range(NCH)]
        dv(g).wait_ge(g.sPE.h, xn_rd)
        ac(g).wait_ge(g.sPE.h, xn_rd)
        emit_ln(g, xin, xn, TC)
        xnm = g.sDV.n
        if s == 0 and tq == 0:
            dump_bf(g, g.scr[:, 0:4096], 4096)

        for oc in range(24):
            i0, i1 = (2 * oc) % 4, (2 * oc + 1) % 4
            sy_ = sy(g)
            sy_.wait_ge(g.sPE.h, slot_mark[i0])
            g.sR[i0].inc(sy_.dma_start(out=gu_slot[i0], in_=g.DW["wgT"][oc].rearrange("p k e -> p (k e)")), 16)
            ac(g).wait_ge(g.sPE.h, slot_mark[i1])
            g.sR[i1].inc(ac(g).dma_start(out=gu_slot[i1], in_=g.DW["wuT"][oc].rearrange("p k e -> p (k e)")), 16)
            wm0, wm1 = g.sR[i0].n, g.sR[i1].n
            b0, b1 = (2 * oc) % 4, (2 * oc + 1) % 4
            psg, psu_ = g.pp[b0], g.pp[b1]
            pe_ = pe(g)
            pe_.wait_ge(g.sR[i0].h, wm0)
            pe_.wait_ge(g.sR[i1].h, wm1)
            pe_.wait_ge(g.sDV.h, xnm)
            pe_.wait_ge(g.sAC.h, gate_bank_rd[b0])
            pe_.wait_ge(g.sDV.h, up_bank_rd[b1])
            for k in range(NCH):
                ins = pe_.matmul(psg, gu_slot[i0][:, 128 * k:128 * (k + 1)], xn[k],
                                 start=(k == 0), stop=(k == NCH - 1))
            g.sPE.inc(ins)
            gm = g.sPE.n
            for k in range(NCH):
                ins = pe_.matmul(psu_, gu_slot[i1][:, 128 * k:128 * (k + 1)], xn[k],
                                 start=(k == 0), stop=(k == NCH - 1))
            g.sPE.inc(ins)
            slot_mark[i0] = slot_mark[i1] = xn_rd = g.sPE.n
            ac_ = ac(g)
            ac_.wait_ge(g.sPE.h, gm)
            ac_.wait_ge(g.sDV.h, sg_rd[oc % 2])
            g.sAC.inc(ac_.activation(g.sg_t[oc % 2][:], psg, AF.Silu, scale=QS))
            gate_bank_rd[b0] = g.sAC.n
            dv_ = dv(g)
            dv_.wait_ge(g.sAC.h, g.sAC.n)
            dv_.wait_ge(g.sPE.h, g.sPE.n)
            dv_.wait_ge(g.sPE.h, h_rd)
            ins = dv_.tensor_tensor(h[:, 512 * oc:512 * (oc + 1)], g.sg_t[oc % 2][:], psu_, ALU.mult)
            g.sDV.inc(ins)
            up_bank_rd[b1] = sg_rd[oc % 2] = g.sDV.n
            if s == 0 and tq == 0 and oc == 0:
                dump_bf(g, g.sg_t[0][:], 512)
                dump_bf(g, h[:, 0:512], 512)
        hm = g.sDV.n

        for po in range(8):
            sl = wd_slot[po % 2]
            sy_ = sy(g)
            sy_.wait_ge(g.sPE.h, wdm[po % 2])
            g.sR[8 + po % 2].inc(sy_.dma_start(out=sl, in_=g.DW["wdT"][po].rearrange("p k e -> p (k e)")), 16)
            wm = g.sR[8 + po % 2].n
            psd = g.pp[4 + po % 2]
            pe_ = pe(g)
            pe_.wait_ge(g.sR[8 + po % 2].h, wm)
            pe_.wait_ge(g.sDV.h, hm)
            pe_.wait_ge(g.sAC.h, down_bank_rd[po % 2])
            for k in range(24):
                ins = pe_.matmul(psd, sl[:, 128 * k:128 * (k + 1)], h[:, 512 * k:512 * (k + 1)],
                                 start=(k == 0), stop=(k == 23))
            g.sPE.inc(ins)
            wdm[po % 2] = h_rd = g.sPE.n
            ac_ = ac(g)
            ac_.wait_ge(g.sPE.h, g.sPE.n)
            ac_.wait_ge(g.sDV.h, acc_rd)
            g.sAC.inc(ac_.activation(g.accb[:, po, :], psd, AF.Copy, scale=QS * QS))
            down_bank_rd[po % 2] = g.sAC.n
            if s == 0 and tq == 0 and po == 0:
                dump_f(g, g.accb[:, 0, :], 512)
                sy(g).wait_ge(g.sR[8].h, wm)
                dump_bf(g, wd_slot[0], 3072)
                dump_bf(g, h[:, 0:4096], 4096)

        accin = [g.accb[:, c, :] for c in range(NCH)]
        dv_ = dv(g)
        dv_.wait_ge(g.sAC.h, g.sAC.n)
        emit_ln(g, accin, None, TC, tmp_bf=tmp_bf, residual=xin)
        acc_rd = g.sDV.n


def emit_l0_attn(g, s):
    scr = g.scr
    xn = [scr[:, 0:4096].rearrange("p (c t) -> p c t", c=NCH)[:, c, :] for c in range(NCH)]
    kvbuf = [(scr[:, 0:1024], scr[:, 1024:2048]), (scr[:, 2048:3072], scr[:, 3072:4096])]
    k_t = [scr[:, 4096 + 512 * c: 4096 + 512 * (c + 1)] for c in range(NCH)]
    v_t = [scr[:, 8192 + 512 * c: 8192 + 512 * (c + 1)] for c in range(NCH)]
    pin = [scr[:, 0:4096].rearrange("p (c t) -> p c t", c=NCH)[:, c, :] for c in range(NCH)]
    ring = [scr[:, 12288 + 1024 * i: 12288 + 1024 * (i + 1)] for i in range(2)]
    g.ln_sq = [g.smb[:, 2048 + 128 * c: 2048 + 128 * (c + 1)] for c in range(NCH)]
    ctx_bf = g.ctxf[:].bitcast(BF16)
    tmp_bf = [ctx_bf[:, 128 * c: 128 * (c + 1)] for c in range(NCH)]
    vk_sb = g.smb[:, 1024:1280]
    ks_bd = g.smb[:, 1280:1536]
    rden_b = g.smb[:, 1536:2048]
    denq = g.smb[:, 4352:4864]
    den_sb = g.smb[:, 4352:4864]
    psA, psB, psC = g.psA, g.psB, g.psC
    psC_bf = psC[:].bitcast(BF16)

    g.vk_acc = g.smf[:, 3072:4096]
    g.ks_acc = g.smf[:, 2561:2569]
    g.vkacc_dv = 0
    g.vk_first = True
    kv_ev = [0, 0]
    kv_rd2 = [0, 0]
    slot_pe = [0, 0]
    qkv_ev = [0, 0]
    kv_rd = 0
    kvT_rd = 0
    kvT_ev = 0
    xn_rd = 0
    started = {}

    def mmacc(pe_, bank_key, out, lhsT, rhs, tp=None):
        st = bank_key not in started
        started[bank_key] = True
        return pe_.matmul(out, lhsT, rhs, start=st, stop=False, skip_group_check=True,
                          tile_position=tp)

    pe_, dv_, ac_, pl_ = pe(g), dv(g), ac(g), pl(g)
    sy(g).wait_ge(g.sR[13].h, g.prep_job_o["d_qkvT"][1])
    ac_.wait_ge(g.sDV.h, g.prep_done_dv)
    pe_.wait_ge(g.sX.h, g.sX.n)

    for tc in range(NTC):
        tsl = slice(TC * tc, TC * (tc + 1))
        dv(g).wait_ge(g.sPE.h, xn_rd)
        ac(g).wait_ge(g.sPE.h, xn_rd)
        for j4 in range(4):
            sl = slice(TC * tc + 128 * j4, TC * tc + 128 * (j4 + 1))
            xin4 = [g.x[:, c, sl] for c in range(NCH)]
            xnsl = [xn[c][:, 128 * j4:128 * (j4 + 1)] for c in range(NCH)]
            emit_ln(g, xin4, xnsl, 128)
        xnm = g.sDV.n
        for oc in range(24):
            sl_i = oc % 2
            sy_ = sy(g)
            sy_.wait_ge(g.sPE.h, slot_pe[sl_i])
            g.sR[sl_i].inc(sy_.dma_start(out=ring[sl_i], in_=g.DW["qkvT"][oc].rearrange("p k e -> p (k e)")), 16)
            wv = g.sR[sl_i].n
            psq = psB if oc % 2 == 0 else psC
            pe_.wait_ge(g.sR[sl_i].h, wv)
            pe_.wait_ge(g.sDV.h, xnm)
            pe_.wait_ge(g.sAC.h, qkv_ev[oc % 2])
            if oc < 2:
                pe_.wait_ge(g.sDV.h, kvT_ev)
            for k in range(NCH):
                ins = pe_.matmul(psq[:], ring[sl_i][:, 128 * k:128 * (k + 1)], xn[k],
                                 start=(k == 0), stop=(k == NCH - 1))
            g.sPE.inc(ins)
            slot_pe[sl_i] = g.sPE.n
            ac_.wait_ge(g.sPE.h, g.sPE.n)
            if oc < 8:
                dst = g.biga[:, 2048 * oc + TC * tc: 2048 * oc + TC * (tc + 1)]
            elif oc < 16:
                ac_.wait_ge(g.sPE.h, kv_rd)
                dst = k_t[oc - 8]
            else:
                ac_.wait_ge(g.sPE.h, kv_rd)
                dst = v_t[oc - 16]
            g.sAC.inc(ac_.activation(dst, psq[:], AF.Relu))
            qkv_ev[oc % 2] = g.sAC.n
        qkvm = g.sAC.n
        if s == 0 and tc == 0:
            dump_bf(g, g.scr[:, 0:4096], 4096)
            dump_bf(g, g.biga[:, 0:512], 512)
            dump_bf(g, k_t[0], 512)
            dump_bf(g, v_t[0], 512)
        def vk_mms(t4p):
            kTp, vTp = kvbuf[t4p % 2]
            pe_.wait_ge(g.sDV.h, kv_ev[t4p % 2])
            pe_.wait_ge(g.sDV.h, g.vkacc_dv)
            for hg in range(8):
                for j in range(4):
                    hcol = 128 * hg + 32 * j
                    ins = pe_.matmul(psA[32 * j:32 * (j + 1), hcol:hcol + 32],
                                     kTp[:, hcol:hcol + 32], vTp[:, hcol:hcol + 32],
                                     start=True, stop=True, skip_group_check=True,
                                     tile_position=(0, 32 * j))
            for hg in range(8):
                for j in range(4):
                    hcol = 128 * hg + 32 * j
                    ins = pe_.matmul(psA[32 * j:32 * (j + 1), 1024 + hg:1024 + hg + 1],
                                     kTp[:, hcol:hcol + 32], g.ones[:, 0:1],
                                     start=True, stop=True, skip_group_check=True,
                                     tile_position=(0, 32 * j))
            g.sPE.inc(ins)
            kv_rd2[t4p % 2] = g.sPE.n
            dv_.wait_ge(g.sPE.h, g.sPE.n)
            if g.vk_first:
                dv_.tensor_copy(g.vk_acc[:], psA[:, 0:1024])
                ins2 = dv_.tensor_copy(g.ks_acc[:], psA[:, 1024:1032])
                g.vk_first = False
            else:
                dv_.tensor_tensor(g.vk_acc[:], g.vk_acc[:], psA[:, 0:1024], ALU.add)
                ins2 = dv_.tensor_tensor(g.ks_acc[:], g.ks_acc[:], psA[:, 1024:1032], ALU.add)
            g.sDV.inc(ins2)
            g.vkacc_dv = g.sDV.n

        for t4 in range(4):
            par = t4 % 2
            kTp, vTp = kvbuf[par]
            pe_.wait_ge(g.sAC.h, qkvm)
            pe_.wait_ge(g.sDV.h, kvT_ev)
            pe_.wait_ge(g.sPE.h, kv_rd2[par])
            for c in range(NCH):
                ins = pe_.transpose(psC_bf[:, 128 * c:128 * (c + 1)],
                                    k_t[c][:, 128 * t4:128 * (t4 + 1)], g.ident[:])
            g.sPE.inc(ins)
            dv_.wait_ge(g.sPE.h, g.sPE.n)
            g.sDV.inc(dv_.tensor_copy(kTp[:], psC_bf[:, 0:1024]))
            kvT_ev = g.sDV.n
            pe_.wait_ge(g.sDV.h, kvT_ev)
            for c in range(NCH):
                ins = pe_.transpose(psC_bf[:, 128 * c:128 * (c + 1)],
                                    v_t[c][:, 128 * t4:128 * (t4 + 1)], g.ident[:])
            g.sPE.inc(ins)
            kv_rd = g.sPE.n
            dv_.wait_ge(g.sPE.h, g.sPE.n)
            g.sDV.inc(dv_.tensor_copy(vTp[:], psC_bf[:, 0:1024]))
            kvT_ev = g.sDV.n
            kv_ev[par] = g.sDV.n
            if t4 > 0:
                vk_mms(t4 - 1)
        vk_mms(3)
        xn_rd = g.sPE.n


    vkm = g.sPE.n
    dv_.wait_ge(g.sPE.h, vkm)
    g.sDV.inc(dv_.memset(ks_bd[:], 0.0))
    dv_.wait_ge(g.sDV.h, g.sDV.n)
    for hg in range(8):
        for j in range(4):
            hcol = 128 * hg + 32 * j
            dv_.tensor_copy(vk_sb[32 * j:32 * (j + 1), 32 * hg:32 * (hg + 1)],
                            g.vk_acc[32 * j:32 * (j + 1), hcol:hcol + 32])
            ins = dv_.tensor_copy(ks_bd[32 * j:32 * (j + 1), 32 * hg + 4 * hg + j:32 * hg + 4 * hg + j + 1],
                                  g.ks_acc[32 * j:32 * (j + 1), hg:hg + 1])
    g.sDV.inc(ins)
    evm = g.sDV.n
    if s == 0:
        dump_bf(g, vk_sb[:], 256)
        dump_bf(g, ks_bd[:], 256)
    g.sDV.inc(dv_.memset(g.smf[:, 2560:2561], 1e-15))
    eps_col = g.smf[:, 2560:2561]
    pe_.wait_ge(g.sDV.h, evm)
    for tc in range(NTC):
        for hg in range(8):
            ins = pe_.matmul(psA[0:32, 512 * tc:512 * (tc + 1)], ks_bd[:, 32 * hg:32 * (hg + 1)],
                             g.biga[:, 2048 * hg + TC * tc: 2048 * hg + TC * (tc + 1)],
                             start=(hg == 0), stop=(hg == 7))
    g.sPE.inc(ins)
    ac_.wait_ge(g.sPE.h, g.sPE.n)
    ac_.wait_ge(g.sDV.h, g.sDV.n)
    for tc4 in range(NTC):
        g.sAC.inc(ac_.activation(g.ctxf[0:32, 0:512], psA[0:32, 512 * tc4:512 * (tc4 + 1)],
                                 AF.Ln, bias=eps_col[0:32, :]))
        ac_.wait_ge(g.sAC.h, g.sAC.n)
        g.sAC.inc(ac_.activation(g.bigb[0:32, 512 * tc4:512 * (tc4 + 1)], g.ctxf[0:32, 0:512],
                                 AF.Exp, scale=-1.0))
        ac_.wait_ge(g.sAC.h, g.sAC.n)
    dv_.wait_ge(g.sAC.h, g.sAC.n)
    g.sDV.inc(dv_.tensor_copy(g.bigb[32:64, 0:2048], g.bigb[0:32, 0:2048]))
    sy_ = sy(g)
    sy_.wait_ge(g.sDV.h, g.sDV.n)
    for tc4 in range(NTC):
        g.sR[11].inc(sy_.dma_start(out=g.DW["den"][:, 512 * tc4:512 * (tc4 + 1)],
                                   in_=g.bigb[32:64, 512 * tc4:512 * (tc4 + 1)]), 16)
    den_st = g.sR[11].n

    if s == 0:
        sy(g).wait_ge(g.sAC.h, g.sAC.n)
        dump_bf(g, g.bigb[0:32, 0:2048], 2048, rows=32)
    projm = 0
    numtt = g.sDV.n
    nq_pe = 0
    accq = 0
    dv_.wait_ge(g.sR[11].h, den_st)
    for tc in range(NTC):
        tsl = slice(TC * tc, TC * (tc + 1))
        for hg in range(8):
            sy_ = sy(g)
            sy_.wait_ge(g.sR[11].h, den_st)
            sy_.wait_ge(g.sPE.h, nq_pe)
            g.sR[10].inc(sy_.dma_start(out=denq[0:4, :], in_=g.DW["den"][4 * hg:4 * (hg + 1), tsl]), 16)
            if s == 0 and tc == 1 and hg == 0:
                sy(g).wait_ge(g.sR[10].h, g.sR[10].n)
                dump_bf(g, denq[0:4, :], 512, rows=4)
                dump_bf(g, g.DW["den"][:, 512:1024], 512, rows=32)
            pe_.wait_ge(g.sDV.h, numtt)
            pe_.wait_ge(g.sAC.h, qkv_ev[0])
            for j in range(4):
                ins = pe_.matmul(psB[32 * j:32 * (j + 1), :], vk_sb[32 * j:32 * (j + 1), 32 * hg:32 * (hg + 1)],
                                 g.biga[32 * j:32 * (j + 1), 2048 * hg + TC * tc: 2048 * hg + TC * (tc + 1)],
                                 start=True, stop=True, skip_group_check=True,
                                 tile_position=(32 * j, 32 * j))
            g.sPE.inc(ins)
            numm = g.sPE.n
            pe_.wait_ge(g.sR[10].h, g.sR[10].n)
            pe_.wait_ge(g.sDV.h, kvT_ev)
            pe_.wait_ge(g.sAC.h, qkv_ev[1])
            for j in range(4):
                ins = pe_.matmul(psC[32 * j:32 * (j + 1), :], g.sel[:, 128 * j:128 * j + 32],
                                 denq[0:4, :], start=True, stop=True, skip_group_check=True,
                                 tile_position=(0, 32 * j))
            g.sPE.inc(ins)
            nq_pe = g.sPE.n
            dv_.wait_ge(g.sPE.h, nq_pe)
            g.sDV.inc(dv_.tensor_copy(rden_b[:], psC[:]))
            kvT_ev = g.sDV.n
            dv_.wait_ge(g.sDV.h, g.sDV.n)
            dv_.wait_ge(g.sPE.h, projm)
            g.sDV.inc(dv_.tensor_tensor(pin[hg], rden_b[:], psB[:], ALU.mult))
            numtt = g.sDV.n
            if s == 0 and tc == 1 and hg == 0:
                dump_bf(g, rden_b[:], 512)
                dump_bf(g, pin[hg], 512)
        pinm = g.sDV.n
        if s == 0 and tc == 1:
            dump_bf(g, g.scr[:, 0:4096], 4096)
        for po in range(8):
            sl_i = po % 2
            sy_ = sy(g)
            sy_.wait_ge(g.sR[13].h, g.prep_job_o["d_projT"][1])
            sy_.wait_ge(g.sPE.h, slot_pe[sl_i])
            g.sR[sl_i].inc(sy_.dma_start(out=ring[sl_i], in_=g.DW["projT"][po].rearrange("p k e -> p (k e)")), 16)
            wv = g.sR[sl_i].n
            psq = psB if po % 2 == 0 else psC
            pe_.wait_ge(g.sR[sl_i].h, wv)
            pe_.wait_ge(g.sDV.h, pinm)
            pe_.wait_ge(g.sAC.h, qkv_ev[po % 2])
            for k in range(NCH):
                ins = pe_.matmul(psq[:], ring[sl_i][:, 128 * k:128 * (k + 1)], pin[k],
                                 start=(k == 0), stop=(k == NCH - 1))
            g.sPE.inc(ins)
            slot_pe[sl_i] = g.sPE.n
            projm = g.sPE.n
            ac_.wait_ge(g.sPE.h, g.sPE.n)
            ac_.wait_ge(g.sDV.h, accq)
            g.sAC.inc(ac_.activation(g.accb[:, po, :], psq[:], AF.Identity,
                                     bias=g.bias_cols[:, po:po + 1]))
            qkv_ev[po % 2] = g.sAC.n
        if s == 0 and tc == 1:
            sy(g).wait_ge(g.sAC.h, g.sAC.n)
            dump_f(g, g.accb[:, 0, :], 512)
        for j4 in range(4):
            sl = slice(TC * tc + 128 * j4, TC * tc + 128 * (j4 + 1))
            xin = [g.x[:, c, sl] for c in range(NCH)]
            accin = [g.accb[:, c, 128 * j4:128 * (j4 + 1)] for c in range(NCH)]
            dv_.wait_ge(g.sAC.h, g.sAC.n)
            emit_ln(g, accin, None, 128, tmp_bf=tmp_bf, residual=xin)
        accq = g.sDV.n


def emit_l1_attn(g, s):
    scr = g.scr
    xn = [scr[:, 0:4096].rearrange("p (c t) -> p c t", c=NCH)[:, c, :] for c in range(NCH)]
    wts = scr[:, 0:4096]
    qsl = lambda gp: scr[:, 4096 + 512 * gp: 4096 + 512 * (gp + 1)]
    ring = [scr[:, 8192 + 2048 * i: 8192 + 2048 * (i + 1)] for i in range(2)]
    wbf = scr[:, 12288:14336]
    vtmp = [g.smb[:, 1024:1536], g.smb[:, 1536:2048]]
    g.ln_sq = [g.smb[:, 2048 + 128 * c: 2048 + 128 * (c + 1)] for c in range(NCH)]
    ctx_bf = g.ctxf[:].bitcast(BF16)
    tmp_bf = [ctx_bf[:, 128 * c: 128 * (c + 1)] for c in range(NCH)]
    ctxwb = ctx_bf
    negmax = g.smf[:, 3072:3076]
    ssum = g.smf[:, 3076:3080]
    rsum = g.smf[:, 3080:3082]
    negmax2 = g.smf[:, 3082:3084]
    negT = g.smf[:, 3084:3086]
    fs = g.smf[:, 3086:3090]
    sf = g.smf[:, 3090:3094]
    psA, psB, psC = g.psA, g.psB, g.psC
    psA_bf = psA[:].bitcast(BF16)
    psC_bf = psC[:].bitcast(BF16)

    sy(g).wait_ge(g.sO.h, g.prep_all_o)
    ac(g).wait_ge(g.sDV.h, g.prep_done_dv)
    slot_pe = [0, 0]
    vt_ev = 0
    vtmp_rd = [0, 0]
    xn_rd = 0
    g.l1a_wts_rd = [0, 0]
    g.l1a_psC_rd = (g.sDV, 0)
    g.l1a_icnt = 0
    g.psA_rd = [[], []]
    g.l1a_was_rot = True
    g.l1a_wb_rd = [0, 0]
    g.l1a_ctx_ev = 0
    g.l1a_proj_ev = 0
    acc_rd = 0

    for tc in range(NTC):
        tsl = slice(TC * tc, TC * (tc + 1))
        dv(g).wait_ge(g.sPE.h, xn_rd)
        ac(g).wait_ge(g.sPE.h, xn_rd)
        for j4 in range(4):
            sl = slice(TC * tc + 128 * j4, TC * tc + 128 * (j4 + 1))
            xin = [g.x[:, c, sl] for c in range(NCH)]
            xnsl = [xn[c][:, 128 * j4:128 * (j4 + 1)] for c in range(NCH)]
            emit_ln(g, xin, xnsl, 128)
        xnm = g.sDV.n

        pe_, dv_, ac_, pl_ = pe(g), dv(g), ac(g), pl(g)
        for gp in range(8):
            for mi, (dwn, bcol) in enumerate((("aq", None), ("ak", 16 + gp), ("av", 24 + gp))):
                sl_i = (3 * gp + mi) % 2
                sy_ = sy(g)
                sy_.wait_ge(g.sPE.h, slot_pe[sl_i])
                g.sR[sl_i].inc(sy_.dma_start(out=ring[sl_i][:, 0:1024], in_=g.DW[dwn][gp].rearrange("p k e -> p (k e)")), 16)
                wv = g.sR[sl_i].n
                pe_.wait_ge(g.sR[sl_i].h, wv)
                pe_.wait_ge(g.sDV.h, xnm)
                pe_.wait_ge(g.sAC.h, g.l1a_proj_ev)
                pe_.wait_ge(g.sDV.h, g.l1a_ctx_ev)
                for k in range(NCH):
                    ins = pe_.matmul(psB[:], ring[sl_i][:, 128 * k:128 * (k + 1)], xn[k],
                                     start=(k == 0), stop=(k == NCH - 1))
                g.sPE.inc(ins)
                slot_pe[sl_i] = g.sPE.n
                ac_.wait_ge(g.sPE.h, g.sPE.n)
                if mi == 0:
                    g.sAC.inc(ac_.activation(qsl(gp), psB[:], AF.Identity, scale=0.125,
                                             bias=g.bias_colsq[:, gp:gp + 1]))
                elif mi == 1:
                    g.sAC.inc(ac_.activation(g.biga[:, 2048 * gp + TC * tc: 2048 * gp + TC * (tc + 1)],
                                             psB[:], AF.Identity, bias=g.bias_cols[:, bcol:bcol + 1]))
                else:
                    vslot = gp % 2
                    ac_.wait_ge(g.sPE.h, vtmp_rd[vslot])
                    g.sAC.inc(ac_.activation(vtmp[vslot][:], psB[:], AF.Identity,
                                             bias=g.bias_cols[:, bcol:bcol + 1]))
                g.l1a_proj_ev = g.sAC.n
            vslot = gp % 2
            pe_.wait_ge(g.sAC.h, g.l1a_proj_ev)
            pe_.wait_ge(g.sDV.h, vt_ev)
            for t4 in range(4):
                ins = pe_.transpose(psC_bf[:, 128 * t4:128 * (t4 + 1)],
                                    vtmp[vslot][:, 128 * t4:128 * (t4 + 1)], g.ident[:])
            g.sPE.inc(ins)
            vtmp_rd[vslot] = g.sPE.n
            dv_.wait_ge(g.sPE.h, g.sPE.n)
            for t4 in range(4):
                ktc = 4 * tc + t4
                ins = dv_.tensor_copy(g.bigb[:, 1024 * ktc + 128 * gp: 1024 * ktc + 128 * (gp + 1)],
                                      psC_bf[:, 128 * t4:128 * (t4 + 1)])
            g.sDV.inc(ins)
            vt_ev = g.sDV.n

        if s == 0 and tc == 0:
            dump_bf(g, g.scr[:, 0:4096], 4096)
            dump_bf(g, qsl(0), 512)
            dump_bf(g, g.biga[:, 0:512], 512)
            dump_bf(g, g.bigb[:, 0:1024], 1024)
        for qi in range(4):
            qig = 4 * tc + qi
            nkt = 128 * (qig + 1)
            nhalf = 2 if nkt > 1024 else 1
            hlen = [min(1024, nkt), nkt - 1024] if nhalf == 2 else [nkt]

            def finish_chain(st):
                h2, wb2, expm = st
                hp = h2 % 2
                dv_.wait_ge(g.sAC.h, expm)
                if nhalf == 1:
                    g.sDV.inc(dv_.reciprocal(rsum[:, hp:hp + 1], ssum[:, 2 * hp:2 * hp + 1]))
                    dv_.wait_ge(g.sDV.h, g.sDV.n)
                    g.sDV.inc(dv_.tensor_scalar(wb2[:, 0:nkt], wb2[:, 0:nkt], rsum[:, hp:hp + 1], None, ALU.mult))
                    return
                nm01 = negmax[:, 2 * hp:2 * hp + 2]
                ss01 = ssum[:, 2 * hp:2 * hp + 2]
                g.sDV.inc(dv_.tensor_tensor(negT[:, hp:hp + 1], nm01[:, 0:1], nm01[:, 1:2], ALU.min))
                dv_.wait_ge(g.sDV.h, g.sDV.n)
                g.sDV.inc(dv_.tensor_scalar(fs[:, 2 * hp:2 * hp + 2], nm01, negT[:, hp:hp + 1], None, ALU.subtract))
                ac_.wait_ge(g.sDV.h, g.sDV.n)
                g.sAC.inc(ac_.activation(fs[:, 2 * hp:2 * hp + 2], fs[:, 2 * hp:2 * hp + 2], AF.Exp, scale=-1.0))
                dv_.wait_ge(g.sAC.h, g.sAC.n)
                g.sDV.inc(dv_.tensor_tensor(sf[:, 2 * hp:2 * hp + 2], ss01, fs[:, 2 * hp:2 * hp + 2], ALU.mult))
                dv_.wait_ge(g.sDV.h, g.sDV.n)
                g.sDV.inc(dv_.reduce_sum(rsum[:, hp:hp + 1], sf[:, 2 * hp:2 * hp + 2], axis=AX.X))
                dv_.wait_ge(g.sDV.h, g.sDV.n)
                g.sDV.inc(dv_.reciprocal(rsum[:, hp:hp + 1], rsum[:, hp:hp + 1]))
                dv_.wait_ge(g.sDV.h, g.sDV.n)
                g.sDV.inc(dv_.tensor_scalar(fs[:, 2 * hp:2 * hp + 2], fs[:, 2 * hp:2 * hp + 2],
                                            rsum[:, hp:hp + 1], None, ALU.mult))
                dv_.wait_ge(g.sDV.h, g.sDV.n)
                dv_.tensor_scalar(wb2[:, 0:1024], wb2[:, 0:1024], fs[:, 2 * hp:2 * hp + 1], None, ALU.mult)
                g.sDV.inc(dv_.tensor_scalar(wb2[:, 1024:nkt], wb2[:, 1024:nkt],
                                            fs[:, 2 * hp + 1:2 * hp + 2], None, ALU.mult))

            def tail_a(st):
                h2, wb2, expm = st
                wp = h2 % 2
                wdst = wts[:, 2048 * wp:2048 * wp + nkt]
                ntile = qig + 1
                wtsem = None
                pe_.wait_ge(g.sDV.h, g.sDV.n)
                pe_.wait_ge(g.sDV.h, vt_ev)
                for b in range(0, ntile, 8):
                    nb = min(8, ntile - b)
                    sm, mk = g.l1a_psC_rd
                    pe_.wait_ge(sm.h, mk)
                    for j5 in range(nb):
                        ins = pe_.transpose(psC_bf[:, 128 * j5:128 * (j5 + 1)],
                                            wb2[:, 128 * (b + j5):128 * (b + j5 + 1)], g.ident[:])
                    g.sPE.inc(ins)
                    if h2 % 2 == 0:
                        dv_.wait_ge(g.sPE.h, g.sPE.n)
                        dv_.wait_ge(g.sPE.h, g.l1a_wts_rd[wp])
                        g.sDV.inc(dv_.tensor_copy(wdst[:, 1024 * (b // 8):1024 * (b // 8) + 128 * nb],
                                                  psC_bf[:, 0:128 * nb]))
                        wtsem = (g.sDV, g.sDV.n)
                    else:
                        ac_.wait_ge(g.sPE.h, g.sPE.n)
                        ac_.wait_ge(g.sPE.h, g.l1a_wts_rd[wp])
                        g.sAC.inc(ac_.activation(wdst[:, 1024 * (b // 8):1024 * (b // 8) + 128 * nb],
                                                 psC_bf[:, 0:128 * nb], AF.Copy))
                        wtsem = (g.sAC, g.sAC.n)
                    g.l1a_psC_rd = wtsem
                g.l1a_wb_rd[h2 % 2] = g.sPE.n
                return (h2, wtsem)

            def tail_b(stb):
                h2, wtsem = stb
                gp2, sub2 = h2 // 2, h2 % 2
                wp = h2 % 2
                pe_.wait_ge(wtsem[0].h, wtsem[1])
                if sub2 == 0:
                    pe_.wait_ge(g.sDV.h, g.l1a_ctx_ev)
                    pe_.wait_ge(g.sAC.h, g.l1a_proj_ev)
                for kt in range(qig + 1):
                    ins = pe_.matmul(psB[64 * sub2:64 * (sub2 + 1), 0:128],
                                     g.bigb[:, 1024 * kt + 128 * gp2 + 64 * sub2: 1024 * kt + 128 * gp2 + 64 * (sub2 + 1)],
                                     wts[:, 2048 * wp + 128 * kt:2048 * wp + 128 * (kt + 1)],
                                     start=(kt == 0), stop=(kt == qig))
                g.sPE.inc(ins)
                g.l1a_wts_rd[wp] = g.sPE.n
                if sub2 == 1:
                    dv_.wait_ge(g.sPE.h, g.sPE.n)
                    g.sDV.inc(dv_.tensor_copy(g.accb[:, gp2, 128 * qi:128 * (qi + 1)], psB[:, 0:128]))
                    g.l1a_ctx_ev = g.sDV.n

            def flush(pa, pb):
                if pb is not None:
                    tail_b(pb)
                if pa is not None:
                    finish_chain(pa)
                    tail_b(tail_a(pa))
                return None, None

            pend_a = None
            pend_b = None
            for h in range(16):
                gp, sub = h // 2, h % 2
                hp = h % 2
                wb = wbf if hp == 0 else ctxwb
                q_ap = qsl(gp)[64 * sub:64 * (sub + 1), 128 * qi:128 * (qi + 1)]
                for hf in range(nhalf):
                    reg = g.l1a_icnt % 2
                    g.l1a_icnt += 1
                    ro = 1024 * reg
                    klo = 1024 * hf
                    ln2 = hlen[hf]
                    for sm, mk in g.psA_rd[reg]:
                        pe_.wait_ge(sm.h, mk)
                    g.psA_rd[reg] = []
                    nm2 = (ln2 + 511) // 512
                    for kb in range(nm2):
                        n0 = 512 * kb
                        nn = min(512, ln2 - n0)
                        ins = pe_.matmul(psA[:, ro + n0: ro + n0 + nn], q_ap,
                                         g.biga[64 * sub:64 * (sub + 1),
                                                2048 * gp + klo + n0: 2048 * gp + klo + n0 + nn],
                                         start=True, stop=True)
                        if kb == nm2 - 2:
                            g.sPE.inc(ins)
                            early = g.sPE.n
                    g.sPE.inc(ins)
                    scm = g.sPE.n
                    last = (hf == nhalf - 1)
                    nmc = negmax[:, 2 * hp + hf:2 * hp + hf + 1]
                    ssc = ssum[:, 2 * hp + hf:2 * hp + hf + 1]
                    if nm2 > 1:
                        n_early = 512 * (nm2 - 1)
                        dv_.wait_ge(g.sPE.h, early)
                        g.sDV.inc(dv_.reduce_max(nmc, psA[:, ro: ro + n_early], axis=AX.X, negate=True))
                        dv_.wait_ge(g.sPE.h, scm)
                        if last:
                            g.sDV.inc(dv_.tensor_tensor(psA[:, ro + ln2 - 128: ro + ln2],
                                                        psA[:, ro + ln2 - 128: ro + ln2], g.cmask[:], ALU.add))
                            dv_.wait_ge(g.sDV.h, g.sDV.n)
                        g.sDV.inc(dv_.reduce_max(negmax2[:, reg:reg + 1], psA[:, ro + n_early: ro + ln2],
                                                 axis=AX.X, negate=True))
                        dv_.wait_ge(g.sDV.h, g.sDV.n)
                        g.sDV.inc(dv_.tensor_tensor(nmc, nmc, negmax2[:, reg:reg + 1], ALU.min))
                    else:
                        dv_.wait_ge(g.sPE.h, scm)
                        if last:
                            g.sDV.inc(dv_.tensor_tensor(psA[:, ro + ln2 - 128: ro + ln2],
                                                        psA[:, ro + ln2 - 128: ro + ln2], g.cmask[:], ALU.add))
                            dv_.wait_ge(g.sDV.h, g.sDV.n)
                        g.sDV.inc(dv_.reduce_max(nmc, psA[:, ro: ro + ln2], axis=AX.X, negate=True))
                    ac_.wait_ge(g.sDV.h, g.sDV.n)
                    ac_.wait_ge(g.sPE.h, g.l1a_wb_rd[hp])
                    g.sAC.inc(ac_.activation(wb[:, klo:klo + ln2], psA[:, ro: ro + ln2], AF.Exp,
                                             bias=nmc, accum_out=ssc))
                    g.psA_rd[reg].append((g.sAC, g.sAC.n))
                st = (h, wb, g.sAC.n)
                nb2 = None
                if pend_a is not None:
                    finish_chain(pend_a)
                    nb2 = tail_a(pend_a)
                if pend_b is not None:
                    tail_b(pend_b)
                pend_a, pend_b = st, nb2
            pend_a, pend_b = flush(pend_a, pend_b)
            sl = slice(TC * tc + 128 * qi, TC * tc + 128 * (qi + 1))
            xin = [g.x[:, c, sl] for c in range(NCH)]
            accin = [g.accb[:, c, 128 * qi:128 * (qi + 1)] for c in range(NCH)]
            dv_.wait_ge(g.sDV.h, g.l1a_ctx_ev)
            emit_ln(g, accin, None, 128, tmp_bf=tmp_bf, residual=xin)
            acc_rd = g.sDV.n
        xn_rd = g.sPE.n


def emit_l0_ffn(g, s):
    scr = g.scr
    xn = [scr[:, 0:4096].rearrange("p (c t) -> p c t", c=NCH)[:, c, :] for c in range(NCH)]
    hsl = lambda hu, k2: scr[:, 4096 + 1024 * hu + 512 * k2: 4096 + 1024 * hu + 512 * (k2 + 1)]
    cbc = [scr[:, 8192 + 512 * j: 8192 + 512 * (j + 1)] for j in range(2)]
    rs_ = [g.bigb[:, 2048 * i:2048 * (i + 1)] for i in range(8)]
    wr_sb = [g.smb[:, 4352 + 16 * k: 4352 + 16 * (k + 1)] for k in range(NCH)]
    cstg = g.smb[:, 4480:4992]
    C = g.ctxf
    e_t, top8, mt = C[:, 0:16], C[:, 16:24], C[:, 24:32]
    mr2, m12, cwf = C[:, 32:48], C[:, 48:64], C[:, 64:80]
    negmax, ssum, rsum = C[:, 80:81], C[:, 81:82], C[:, 82:83]
    g.ln_sq = [g.biga[:, 512 * c:512 * (c + 1)] for c in range(NCH)]
    tmp_bf = [g.biga[:, 4096 + 512 * c: 4096 + 512 * (c + 1)] for c in range(NCH)]
    pp = g.pp
    ps_bf16 = pp[5].bitcast(BF16)

    sy(g).wait_ge(g.sO.h, g.prep_all_o)
    g.sR[12].inc(sy(g).dma_start(out=g.smb[:, 4352:4480], in_=g.DW["wr"].rearrange("p k e -> p (k e)")), 16)
    wrm = g.sR[12].n

    slot_pe = [0] * 8
    gate_rd = [0, 0]
    up_rd = [0, 0]
    sg_rd = [0, 0]
    down_ev = [(g.sAC, 0), (g.sAC, 0)]
    pp5_rd = 0
    acc_rd = 0
    xn_rd = 0
    h_rd = 0
    cwq_pe = 0
    cbc_rd = 0
    cstg_rd = 0

    for tq in range(NTC):
        tsl = slice(TC * tq, TC * (tq + 1))
        xin = [g.x[:, c, tsl] for c in range(NCH)]
        dv(g).wait_ge(g.sPE.h, xn_rd)
        ac(g).wait_ge(g.sPE.h, xn_rd)
        emit_ln(g, xin, xn, TC)
        xnm = g.sDV.n

        pe_, dv_, ac_, pl_ = pe(g), dv(g), ac(g), pl(g)
        pe_.wait_ge(g.sDV.h, g.ps_bc_rd_dv)
        pe_.wait_ge(g.sDV.h, xnm)
        pe_.wait_ge(g.sR[12].h, wrm)
        for j in range(4):
            for k in range(NCH):
                ins = pe_.matmul(g.ps_bc[:, 16 * j:16 * (j + 1)],
                                 xn[k][:, 128 * j:128 * (j + 1)], wr_sb[k],
                                 start=(k == 0), stop=(k == NCH - 1))
        g.sPE.inc(ins)
        rmark = g.sPE.n
        for j in range(4):
            lg = g.ps_bc[:, 16 * j:16 * (j + 1)]
            dv_.wait_ge(g.sPE.h, rmark)
            g.sDV.inc(dv_.reduce_max(negmax, lg, axis=AX.X, negate=True))
            ac_.wait_ge(g.sDV.h, g.sDV.n)
            g.sAC.inc(ac_.activation(e_t, lg, AF.Exp, bias=negmax))
            dv_.wait_ge(g.sAC.h, g.sAC.n)
            g.sDV.inc(dv_.reduce_sum(ssum, e_t, axis=AX.X))
            dv_.wait_ge(g.sDV.h, g.sDV.n)
            g.sDV.inc(dv_.reciprocal(rsum, ssum))
            dv_.wait_ge(g.sDV.h, g.sDV.n)
            g.sDV.inc(dv_.max(top8, e_t))
            dv_.wait_ge(g.sDV.h, g.sDV.n)
            g.sDV.inc(dv_.memset(mt, -1.0))
            dv_.wait_ge(g.sDV.h, g.sDV.n)
            g.sDV.inc(dv_.tensor_copy(mt[:, 0:2], top8[:, 0:2]))
            dv_.wait_ge(g.sDV.h, g.sDV.n)
            g.sDV.inc(dv_.match_replace(mr2, mt, e_t, -7.0))
            dv_.wait_ge(g.sDV.h, g.sDV.n)
            g.sDV.inc(dv_.tensor_single_scalar(m12, mr2, -7.0, ALU.is_equal))
            dv_.wait_ge(g.sDV.h, g.sDV.n)
            g.sDV.inc(dv_.tensor_mul(cwf, m12, e_t))
            dv_.wait_ge(g.sDV.h, g.sDV.n)
            g.sDV.inc(dv_.tensor_scalar(g.rt[:, 16 * j:16 * (j + 1)], cwf, rsum, None, ALU.mult))
        g.ps_bc_rd_dv = g.sDV.n
        rchain = g.sDV.n
        pe_.wait_ge(g.sDV.h, rchain)
        sm, vv = down_ev[1]
        pe_.wait_ge(sm.h, vv)
        pe_.wait_ge(g.sDV.h, pp5_rd)
        for j in range(4):
            ins = pe_.transpose(ps_bf16[0:16, 128 * j:128 * (j + 1)], g.rt[:, 16 * j:16 * (j + 1)], g.ident[:])
        g.sPE.inc(ins)
        dv_.wait_ge(g.sPE.h, g.sPE.n)
        dv_.wait_ge(g.sR[11].h, cstg_rd)
        g.sDV.inc(dv_.tensor_copy(cstg[0:16, :], ps_bf16[0:16, 0:512]))
        pp5_rd = g.sDV.n
        sy_ = sy(g)
        sy_.wait_ge(g.sDV.h, g.sDV.n)
        g.sR[11].inc(sy_.dma_start(out=g.DW["cw"][:, tsl], in_=cstg[0:16, :]), 16)
        cw_st = g.sR[11].n
        cstg_rd = cw_st

        for grp in range(9):
            shared = grp == 8
            if not shared:
                e0 = 2 * grp
                if grp % 2 == 0:
                    sy_ = sy(g)
                    sy_.wait_ge(g.sR[11].h, cw_st)
                    sy_.wait_ge(g.sPE.h, cwq_pe)
                    g.sR[10].inc(sy_.dma_start(out=g.smb[0:4, 2560:3072],
                                               in_=g.DW["cw"][4 * (grp // 2):4 * (grp // 2) + 4, tsl]), 16)
                for jj in range(2):
                    j = (e0 + jj) % 4
                    pe_.wait_ge(g.sR[10].h, g.sR[10].n)
                    pe_.wait_ge(g.sDV.h, g.ps_bc_rd_dv)
                    pe_.wait_ge(g.sDV.h, cbc_rd)
                    ins = pe_.matmul(g.ps_bc[:, 512:1024], g.sel[:, 128 * j:128 * (j + 1)],
                                     g.smb[0:4, 2560:3072], start=True, stop=True)
                    g.sPE.inc(ins)
                    cwq_pe = g.sPE.n
                    dv_.wait_ge(g.sPE.h, g.sPE.n)
                    dv_.wait_ge(g.sDV.h, cbc_rd)
                    g.sDV.inc(dv_.tensor_copy(cbc[jj], g.ps_bc[:, 512:1024]))
                    g.ps_bc_rd_dv = g.sDV.n
            cbc_dv = g.sDV.n
            for hu in range(4):
                if shared:
                    j = hu
                    wgsrc = g.DW["sg"][:, :, 256 * j:256 * (j + 1)]
                    wusrc = g.DW["su"][:, :, 256 * j:256 * (j + 1)]
                    wdsrc = g.DW["sd"][:, 2 * j:2 * (j + 1), :]
                else:
                    e = e0 + hu // 2
                    half = hu % 2
                    wgsrc = g.DW["ewg"][e][:, :, 256 * half:256 * (half + 1)]
                    wusrc = g.DW["ewu"][e][:, :, 256 * half:256 * (half + 1)]
                    wdsrc = g.DW["ewd"][e][:, 2 * half:2 * (half + 1), :]
                s0, s1, s2 = (2 * hu) % 4, (2 * hu + 1) % 4, 4 + hu
                sy_ = sy(g)
                sy_.wait_ge(g.sPE.h, slot_pe[s0])
                g.sR[s0].inc(sy_.dma_start(out=rs_[s0].rearrange("p (k o) -> p k o", o=256), in_=wgsrc), 16)
                w0 = g.sR[s0].n
                sy_.wait_ge(g.sPE.h, slot_pe[s1])
                g.sR[s1].inc(sy_.dma_start(out=rs_[s1].rearrange("p (k o) -> p k o", o=256), in_=wusrc), 16)
                w1 = g.sR[s1].n
                sy_.wait_ge(g.sPE.h, slot_pe[s2])
                g.sR[s2].inc(sy_.dma_start(out=rs_[s2].rearrange("p (k o) -> p k o", o=1024), in_=wdsrc), 16)
                pe_.wait_ge(g.sR[s0].h, w0)
                pe_.wait_ge(g.sDV.h, xnm)
                for po2 in range(2):
                    pe_.wait_ge(g.sAC.h, gate_rd[po2])
                    for k in range(NCH):
                        ins = pe_.matmul(pp[po2], rs_[s0][:, 256 * k + 128 * po2: 256 * k + 128 * (po2 + 1)],
                                         xn[k], start=(k == 0), stop=(k == NCH - 1))
                    g.sPE.inc(ins)
                    ac_.wait_ge(g.sPE.h, g.sPE.n)
                    ac_.wait_ge(g.sDV.h, sg_rd[po2])
                    g.sAC.inc(ac_.activation(g.sg_t[po2][:], pp[po2], AF.Silu))
                    gate_rd[po2] = g.sAC.n
                slot_pe[s0] = g.sPE.n
                pe_.wait_ge(g.sR[s1].h, w1)
                for po2 in range(2):
                    pe_.wait_ge(g.sDV.h, up_rd[po2])
                    for k in range(NCH):
                        ins = pe_.matmul(pp[2 + po2], rs_[s1][:, 256 * k + 128 * po2: 256 * k + 128 * (po2 + 1)],
                                         xn[k], start=(k == 0), stop=(k == NCH - 1))
                    g.sPE.inc(ins)
                    dv_.wait_ge(g.sPE.h, g.sPE.n)
                    dv_.wait_ge(g.sAC.h, gate_rd[po2])
                    dv_.wait_ge(g.sPE.h, h_rd)
                    g.sDV.inc(dv_.tensor_tensor(hsl(hu, po2), g.sg_t[po2][:], pp[2 + po2], ALU.mult))
                    up_rd[po2] = sg_rd[po2] = g.sDV.n
                slot_pe[s1] = g.sPE.n
                if not shared:
                    dv_.wait_ge(g.sDV.h, g.sDV.n)
                    for po2 in range(2):
                        ins = dv_.tensor_tensor(hsl(hu, po2), hsl(hu, po2), cbc[hu // 2], ALU.mult)
                    g.sDV.inc(ins)
            hm_dv = hm_pl = g.sDV.n
            cbc_rd = g.sDV.n
            for po in range(8):
                psd = pp[4 + po % 2]
                pe_.wait_ge(g.sDV.h, hm_dv)
                sm, vv = down_ev[po % 2]
                pe_.wait_ge(sm.h, vv)
                if po % 2 == 1:
                    pe_.wait_ge(g.sDV.h, pp5_rd)
                for hu in range(4):
                    pe_.wait_ge(g.sR[4 + hu].h, g.sR[4 + hu].n)
                    for k2 in range(2):
                        ins = pe_.matmul(psd, rs_[4 + hu][:, 1024 * k2 + 128 * po: 1024 * k2 + 128 * (po + 1)],
                                         hsl(hu, k2), start=(hu == 0 and k2 == 0), stop=(hu == 3 and k2 == 1))
                g.sPE.inc(ins)
                if po >= 6:
                    for hu in range(4):
                        slot_pe[4 + hu] = g.sPE.n
                if grp == 0:
                    ac_.wait_ge(g.sPE.h, g.sPE.n)
                    ac_.wait_ge(g.sDV.h, acc_rd)
                    g.sAC.inc(ac_.activation(g.accb[:, po, :], psd, AF.Copy))
                    down_ev[po % 2] = (g.sAC, g.sAC.n)
                else:
                    dv_.wait_ge(g.sPE.h, g.sPE.n)
                    dv_.wait_ge(g.sAC.h, g.sAC.n if grp == 1 else 0)
                    g.sDV.inc(dv_.tensor_tensor(g.accb[:, po, :], g.accb[:, po, :], psd, ALU.add))
                    down_ev[po % 2] = (g.sDV, g.sDV.n)
            h_rd = g.sPE.n

        dv_.wait_ge(g.sDV.h, g.sDV.n)
        sm, vv = down_ev[0]
        dv_.wait_ge(sm.h, vv)
        sm, vv = down_ev[1]
        dv_.wait_ge(sm.h, vv)
        accin = [g.accb[:, c, :] for c in range(NCH)]
        emit_ln(g, accin, None, TC, tmp_bf=tmp_bf, residual=xin)
        acc_rd = g.sDV.n
        xn_rd = g.sPE.n


def make_consts():
    ones = np.ones((128, 128), dtype=ml_dtypes.bfloat16)
    ident = np.eye(128, dtype=ml_dtypes.bfloat16)
    cmask = np.triu(np.full((128, 128), -1e30, dtype=np.float32), k=1)
    cmaskb = np.triu(np.full((128, 128), -1e30, dtype=ml_dtypes.bfloat16), k=1)
    sel = np.zeros((4, 512), dtype=ml_dtypes.bfloat16)
    for j in range(4):
        sel[j, 128 * j:128 * (j + 1)] = 1.0
    return ones, ident, cmask, cmaskb, sel


_WMAP = {
    "l0a_qkv": "L0_attn_qkv_weight", "l0a_proj": "L0_attn_proj_weight", "l0a_bias": "L0_attn_proj_bias",
    "l0f_wr": "L0_ffn_Wr", "l0f_wg": "L0_ffn_Wg", "l0f_wu": "L0_ffn_Wu", "l0f_wd": "L0_ffn_Wd",
    "l0f_sg": "L0_ffn_Sg", "l0f_su": "L0_ffn_Su", "l0f_sd": "L0_ffn_Sd",
    "l1a_wq": "L1_attn_Wq", "l1a_wk": "L1_attn_Wk", "l1a_wv": "L1_attn_Wv",
    "l1a_bq": "L1_attn_bq", "l1a_bk": "L1_attn_bk", "l1a_bv": "L1_attn_bv",
    "l1f_wg": "L1_ffn_w_gate", "l1f_wu": "L1_ffn_w_up", "l1f_wd": "L1_ffn_w_down",
}


def kernel(**inputs):
    x = np.asarray(inputs["L0_attn_x"])
    xt = np.ascontiguousarray(x.transpose(0, 2, 1))
    ones, ident, cmask, cmaskb, sel = make_consts()
    nc = build()
    in_maps = []
    for c in range(8):
        m = {"x_in": np.ascontiguousarray(xt[2 * c:2 * c + 2]),
             "c_ones": ones, "c_id": ident, "c_mask": cmask, "c_maskb": cmaskb, "c_sel": sel}
        for k, v in _WMAP.items():
            m[k] = np.ascontiguousarray(np.asarray(inputs[v]))
        in_maps.append(m)
    res = run_bass_kernel_spmd(nc, in_maps, core_ids=list(range(8)))
    import kernel as _K
    _K.LAST_DBG = (np.asarray(res.results[0]["dbg_bf"]), np.asarray(res.results[0]["dbg_f"]))
    out = np.concatenate([r["x_out"] for r in res.results], axis=0)
    return np.ascontiguousarray(out.transpose(0, 2, 1)).astype(np.float32)


# version 23
# speedup vs baseline: 1.2593x; 1.2593x over previous
import numpy as np
import ml_dtypes
import concourse.bass as bass
import concourse.mybir as mybir
from concourse.bass_utils import run_bass_kernel_spmd
from contextlib import ExitStack

F32 = mybir.dt.float32
BF16 = mybir.dt.bfloat16
I32 = mybir.dt.int32
AF = mybir.ActivationFunctionType
ALU = mybir.AluOpType
AX = mybir.AxisListType

D = 1024
NCH = 8
S = 2048
TC = 512
NTC = S // TC
EPS = 1e-5
QS = 1.0 / 127.0

SUB = {"l0a": True, "l0f": True, "l1a": True, "l1f": True}
DBG = False
NSEQ = 2


class Sem:
    def __init__(self, nc, stack, name):
        self.h = stack.enter_context(nc.semaphore(name))
        self.n = 0

    def inc(self, ins, k=1):
        ins.then_inc(self.h, k)
        self.n += k
        return ins


class G:
    pass


def build():
    nc = bass.Bass("TRN2", target_bir_lowering=False)
    g = G()
    g.nc = nc

    x_in = nc.dram_tensor("x_in", [NSEQ, D, S], F32, kind="ExternalInput").ap()
    x_out = nc.dram_tensor("x_out", [NSEQ, D, S], F32, kind="ExternalOutput").ap()
    W = {}
    for name, shape, dt in [
        ("l0a_qkv", [3 * D, D], F32), ("l0a_proj", [D, D], F32), ("l0a_bias", [D], F32),
        ("l0f_wr", [D, 16], F32),
        ("l0f_wg", [16, D, 512], F32), ("l0f_wu", [16, D, 512], F32), ("l0f_wd", [16, 512, D], F32),
        ("l0f_sg", [D, D], F32), ("l0f_su", [D, D], F32), ("l0f_sd", [D, D], F32),
        ("l1a_wq", [16, D, 64], F32), ("l1a_wk", [16, D, 64], F32), ("l1a_wv", [16, D, 64], F32),
        ("l1a_bq", [16, 64], F32), ("l1a_bk", [16, 64], F32), ("l1a_bv", [16, 64], F32),
        ("l1f_wg", [3072, D], I32), ("l1f_wu", [3072, D], I32), ("l1f_wd", [D, 3072], I32),
    ]:
        W[name] = nc.dram_tensor(name, shape, dt, kind="ExternalInput").ap()
    g.W = W
    c_ones = nc.dram_tensor("c_ones", [128, 128], BF16, kind="ExternalInput").ap()
    c_id = nc.dram_tensor("c_id", [128, 128], BF16, kind="ExternalInput").ap()
    c_mask = nc.dram_tensor("c_mask", [128, 128], F32, kind="ExternalInput").ap()
    c_sel = nc.dram_tensor("c_sel", [4, 512], BF16, kind="ExternalInput").ap()
    g.dbg_bf = nc.dram_tensor("dbg_bf", [128, 16384], BF16, kind="ExternalOutput").ap()
    g.dbg_f = nc.dram_tensor("dbg_f", [128, 4096], F32, kind="ExternalOutput").ap()
    g.dbg_used = [0, 0]

    DW = {}
    DW["qkvT"] = nc.dram_tensor("d_qkvT", [24, 128, NCH, 128], BF16).ap()
    DW["projT"] = nc.dram_tensor("d_projT", [8, 128, NCH, 128], BF16).ap()
    DW["wgT"] = nc.dram_tensor("d_wgT", [24, 128, NCH, 128], BF16).ap()
    DW["wuT"] = nc.dram_tensor("d_wuT", [24, 128, NCH, 128], BF16).ap()
    DW["wdT"] = nc.dram_tensor("d_wdT", [8, 128, 24, 128], BF16).ap()
    DW["ewg"] = nc.dram_tensor("d_ewg", [16, 128, NCH, 512], BF16).ap()
    DW["ewu"] = nc.dram_tensor("d_ewu", [16, 128, NCH, 512], BF16).ap()
    DW["ewd"] = nc.dram_tensor("d_ewd", [16, 128, 4, 1024], BF16).ap()
    DW["sg"] = nc.dram_tensor("d_sg", [128, NCH, 1024], BF16).ap()
    DW["su"] = nc.dram_tensor("d_su", [128, NCH, 1024], BF16).ap()
    DW["sd"] = nc.dram_tensor("d_sd", [128, NCH, 1024], BF16).ap()
    DW["wr"] = nc.dram_tensor("d_wr", [128, NCH, 16], BF16).ap()
    DW["aq"] = nc.dram_tensor("d_aq", [8, 128, NCH, 128], BF16).ap()
    DW["ak"] = nc.dram_tensor("d_ak", [8, 128, NCH, 128], BF16).ap()
    DW["av"] = nc.dram_tensor("d_av", [8, 128, NCH, 128], BF16).ap()
    DW["den"] = nc.dram_tensor("d_den", [32, S], BF16).ap()
    DW["cw"] = nc.dram_tensor("d_cw", [16, S], BF16).ap()
    g.DW = DW

    with ExitStack() as stack:
        def sb(name, shape, dtype):
            return stack.enter_context(nc.sbuf_tensor(name, shape, dtype))

        g.x = sb("x_res", [128, NCH, S], F32)
        g.scr = sb("scratch", [128, 14336], BF16)
        g.biga = sb("biga", [128, 16384], BF16)
        g.bigb = sb("bigb", [128, 16384], BF16)
        g.accb = sb("accb", [128, NCH, TC], F32)
        g.smf = sb("sm_f32", [128, 4096], F32)
        g.smb = sb("sm_bf", [128, 5120], BF16)
        g.ones = sb("ones", [128, 128], BF16)
        g.ident = sb("ident", [128, 128], BF16)
        g.cmask = sb("cmask", [128, 128], F32)
        g.sel = sb("sel", [4, 512], BF16)
        g.bias_cols = sb("bias_cols", [128, 32], F32)
        g.bias_colsq = sb("bias_colsq", [128, 8], F32)

        g.Rb = g.smb[:, 0:512]
        g.MRb = g.smb[:, 512:1024]
        g.sg_t = [g.smb[:, 1024:1536], g.smb[:, 1536:2048]]
        g.rq = g.smb[:, 2048:2560]
        g.cwq = g.smb[:, 2560:3072]
        g.rt = g.smb[:, 3072:3328]
        g.lnrb = g.smb[:, 3328:4352]
        g.smb_spare = g.smb[:, 4352:5120]
        g.ctxf = g.smf[:, 2048:3072]

        ps = lambda n, shp: stack.enter_context(nc.psum_tensor(n, shp, F32))
        g.psA = ps("ps_A", [128, 2048])
        g.psB = ps("ps_B", [128, 512])
        g.psC = ps("ps_C", [128, 512])
        g.ps_bc = ps("ps_bc", [128, 1024])
        g.pp = [g.psA[:, 512 * i:512 * (i + 1)] for i in range(4)] + [g.psB[:], g.psC[:]]

        g.sW = Sem(nc, stack, "sW")
        g.sX = Sem(nc, stack, "sX")
        g.sXC = Sem(nc, stack, "sXC")
        g.sO = Sem(nc, stack, "sO")
        g.sPE = Sem(nc, stack, "sPE")
        g.sDV = Sem(nc, stack, "sDV")
        g.sAC = Sem(nc, stack, "sAC")
        g.sPL = Sem(nc, stack, "sPL")
        g.sR = [Sem(nc, stack, f"sR{i}") for i in range(16)]

        g.ps_bc_rd_dv = 0
        g.ps_bc_rd_misc = []
        g.lnsq_rd_pe = 0
        g.rb_rd = []
        g.pp_wr_gate = [[] for _ in range(6)]

        block = stack.enter_context(nc.Block())

        @block.sync
        def _(sync):
            emit_all(g, x_in, x_out, c_ones, c_id, c_mask, c_sel)

    return nc


def pe(g): return g.nc.tensor
def dv(g): return g.nc.vector
def ac(g): return g.nc.scalar
def pl(g): return g.nc.gpsimd
def sy(g): return g.nc.sync


def dump_bf(g, ap, n, rows=128):
    if not DBG:
        return
    off = g.dbg_used[0]
    g.dbg_used[0] += n
    sy(g).wait_ge(g.sDV.h, g.sDV.n)
    sy(g).wait_ge(g.sPL.h, g.sPL.n)
    sy(g).wait_ge(g.sAC.h, g.sAC.n)
    g.sO.inc(sy(g).dma_start(out=g.dbg_bf[0:rows, off:off + n], in_=ap), 16)


def dump_f(g, ap, n):
    if not DBG:
        return
    off = g.dbg_used[1]
    g.dbg_used[1] += n
    sy(g).wait_ge(g.sDV.h, g.sDV.n)
    sy(g).wait_ge(g.sPL.h, g.sPL.n)
    sy(g).wait_ge(g.sAC.h, g.sAC.n)
    g.sO.inc(sy(g).dma_start(out=g.dbg_f[:, off:off + n], in_=ap), 16)


def barrier(g):
    sems = [g.sW, g.sX, g.sXC, g.sO, g.sPE, g.sDV, g.sAC, g.sPL] + g.sR
    for eng in (pe(g), dv(g), ac(g), pl(g), sy(g)):
        for s in sems:
            if s.n:
                eng.wait_ge(s.h, s.n)


def emit_all(g, x_in, x_out, c_ones, c_id, c_mask, c_sel):
    g.sXC.inc(pl(g).dma_start(out=g.ones[:], in_=c_ones), 16)
    g.sXC.inc(pl(g).dma_start(out=g.ident[:], in_=c_id), 16)
    g.sXC.inc(pl(g).dma_start(out=g.cmask[:], in_=c_mask), 16)
    g.sXC.inc(pl(g).dma_start(out=g.sel[:], in_=c_sel), 16)
    for c in range(NCH):
        g.sX.inc(pl(g).dma_start(out=g.x[:, c, :], in_=x_in[0, 128 * c:128 * (c + 1), :]), 16)

    emit_prep(g)

    xst = 0
    for s in range(NSEQ):
        if s > 0:
            pl(g).wait_ge(g.sO.h, xst)
            for c in range(NCH):
                g.sX.inc(pl(g).dma_start(out=g.x[:, c, :], in_=x_in[s, 128 * c:128 * (c + 1), :]), 16)
        dv(g).wait_ge(g.sX.h, g.sX.n)
        ac(g).wait_ge(g.sX.h, g.sX.n)

        if SUB["l0a"]:
            emit_l0_attn(g, s)
            barrier(g)
        if SUB["l0f"]:
            emit_l0_ffn(g, s)
            barrier(g)
        if SUB["l1a"]:
            emit_l1_attn(g, s)
            barrier(g)
        if SUB["l1f"]:
            emit_l1_ffn(g, s)
            barrier(g)

        for c in range(NCH):
            g.sO.inc(sy(g).dma_start(out=x_out[s, 128 * c:128 * (c + 1), :], in_=g.x[:, c, :]), 16)
        xst = g.sO.n
    sy(g).wait_ge(g.sO.h, g.sO.n)


def emit_prep(g):
    nc = g.nc
    W, DW = g.W, g.DW
    stage_in = g.scr[:, 0:3072]
    stage_out = g.scr[:, 3072:6144]
    jobs = [
        (W["l0a_qkv"], DW["qkvT"], 24, NCH),
        (W["l0a_proj"], DW["projT"], 8, NCH),
        (W["l1f_wg"], DW["wgT"], 24, NCH),
        (W["l1f_wu"], DW["wuT"], 24, NCH),
        (W["l1f_wd"], DW["wdT"], 8, 24),
    ]
    pe(g).wait_ge(g.sXC.h, g.sXC.n)
    g.prep_job_o = {}
    so_stage = 0
    ps_bf = g.ps_bc[:].bitcast(BF16)
    for src, dst, n_ocb, n_k in jobs:
        kin = n_k * 128
        for ocb in range(n_ocb):
            pl(g).wait_ge(g.sPE.h, g.sPE.n)
            ins = pl(g).dma_start(out=stage_in[:, 0:kin], in_=src[128 * ocb:128 * (ocb + 1), :])
            g.sX.inc(ins, 16)
            pe_ = pe(g)
            pe_.wait_ge(g.sX.h, g.sX.n)
            pe_.wait_ge(g.sDV.h, g.sDV.n)
            for k in range(n_k):
                kk = k % 16
                ins = pe_.transpose(ps_bf[:, 128 * kk:128 * (kk + 1)],
                                    stage_in[:, 128 * k:128 * (k + 1)], g.ident[:])
                if kk == 15 or k == n_k - 1:
                    g.sPE.inc(ins)
                    dv_ = dv(g)
                    dv_.wait_ge(g.sPE.h, g.sPE.n)
                    dv_.wait_ge(g.sO.h, so_stage)
                    lo = 128 * (k - kk)
                    ins2 = dv_.tensor_copy(stage_out[:, lo:128 * (k + 1)], ps_bf[:, 0:128 * (kk + 1)])
                    g.sDV.inc(ins2)
                    if k != n_k - 1:
                        pe_.wait_ge(g.sDV.h, g.sDV.n)
            sy_ = sy(g)
            sy_.wait_ge(g.sDV.h, g.sDV.n)
            ins = sy_.dma_start(out=dst[ocb].rearrange("p k e -> p (k e)"), in_=stage_out[:, 0:kin])
            g.sO.inc(ins, 16)
            so_stage = g.sO.n
            g.prep_job_o[dst.name] = g.sO.n

    g.prep_tr_o = g.sO.n
    pl_ = sy(g)
    g.sX.inc(pl_.dma_start(out=g.smf[0:1, 0:1024], in_=W["l0a_bias"].unsqueeze(0)), 16)
    g.sX.inc(pl_.dma_start(out=g.smf[0:1, 1024:2048], in_=W["l1a_bq"].rearrange("h e -> (h e)").unsqueeze(0)), 16)
    g.sX.inc(pl_.dma_start(out=g.smf[0:1, 2048:3072], in_=W["l1a_bk"].rearrange("h e -> (h e)").unsqueeze(0)), 16)
    g.sX.inc(pl_.dma_start(out=g.smf[0:1, 3072:4096], in_=W["l1a_bv"].rearrange("h e -> (h e)").unsqueeze(0)), 16)
    g.sDV.inc(dv(g).memset(g.accb[0:1, 0, 0:1], 1.0))
    one_f32 = g.accb[0:1, 0, 0:1]
    pe_ = pe(g)
    pe_.wait_ge(g.sX.h, g.sX.n)
    pe_.wait_ge(g.sDV.h, g.sDV.n)
    for c in range(32):
        ins = pe_.matmul(g.ps_bc[:, c:c + 1], g.smf[0:1, 128 * c:128 * (c + 1)], one_f32,
                         start=True, stop=True)
    g.sPE.inc(ins)
    dv_ = dv(g)
    dv_.wait_ge(g.sPE.h, g.sPE.n)
    g.sDV.inc(dv_.tensor_copy(g.bias_cols[:], g.ps_bc[:, 0:32]))
    dv_.wait_ge(g.sDV.h, g.sDV.n)
    g.sDV.inc(dv_.tensor_scalar(g.bias_colsq[:], g.bias_cols[:, 8:16], 0.125, None, ALU.mult))
    g.prep_done_dv = g.sDV.n
    cj = []
    for e in range(16):
        cj.append((W["l0f_wg"][e].rearrange("(k p) o -> p k o", p=128), DW["ewg"][e]))
        cj.append((W["l0f_wu"][e].rearrange("(k p) o -> p k o", p=128), DW["ewu"][e]))
        cj.append((W["l0f_wd"][e].rearrange("(k p) o -> p k o", p=128), DW["ewd"][e]))
    for nm, dnm in (("l0f_sg", "sg"), ("l0f_su", "su"), ("l0f_sd", "sd")):
        r = W[nm].rearrange("(k p) o -> p k o", p=128)
        cj.append((r[:, 0:4, :], DW[dnm][:, 0:4, :]))
        cj.append((r[:, 4:8, :], DW[dnm][:, 4:8, :]))
    cj.append((W["l0f_wr"].rearrange("(k p) o -> p k o", p=128), DW["wr"]))
    for gp in range(8):
        for nm, dst in (("l1a_wq", "aq"), ("l1a_wk", "ak"), ("l1a_wv", "av")):
            for j in range(2):
                cj.append((W[nm][2 * gp + j].rearrange("(k p) e -> p k e", p=128),
                           DW[dst][gp].rearrange("p k (h e) -> p k h e", h=2)[:, :, j, :]))
    cstage = [g.scr[:, 6144 + 4096 * i: 6144 + 4096 * (i + 1)] for i in range(2)]
    cmark = [0, 0]
    for i, (src, dst) in enumerate(cj):
        n = int(np.prod(src.shape[1:]))
        slot = cstage[i % 2][:, 0:n]
        pl(g).wait_ge(g.sO.h, cmark[i % 2])
        ins = pl(g).dma_start(out=slot, in_=src)
        g.sX.inc(ins, 16)
        pl(g).wait_ge(g.sX.h, g.sX.n)
        ins = pl(g).dma_start(out=dst, in_=slot)
        g.sO.inc(ins, 16)
        cmark[i % 2] = g.sO.n


    g.ps_bc_rd_dv = g.sDV.n



def emit_ln(g, in_f32, out_bf, L, tmp_bf=None, residual=None):
    dst = out_bf if residual is None else tmp_bf
    dv_, ac_, pe_ = dv(g), ac(g), pe(g)

    for c in range(NCH):
        ins = dv_.tensor_copy(dst[c], in_f32[c])
    g.sDV.inc(ins)
    castm = g.sDV.n
    ac_.wait_ge(g.sDV.h, castm)
    ac_.wait_ge(g.sPE.h, g.lnsq_rd_pe)
    for c in range(NCH):
        ins = ac_.activation(g.ln_sq[c], dst[c], AF.Square)
    g.sAC.inc(ins)
    sqm = g.sAC.n
    pe_.wait_ge(g.sDV.h, max(castm, g.ps_bc_rd_dv))
    for sm, mk in g.ps_bc_rd_misc:
        pe_.wait_ge(sm.h, mk)
    g.ps_bc_rd_misc = []
    for c in range(NCH):
        pe_.matmul(g.ps_bc[0:1, 0:L], g.ones[:, 0:1], dst[c], start=(c == 0), stop=(c == NCH - 1))
    pe_.wait_ge(g.sAC.h, sqm)
    for c in range(NCH):
        ins = pe_.matmul(g.ps_bc[0:1, 512:512 + L], g.ones[:, 0:1], g.ln_sq[c],
                         start=(c == 0), stop=(c == NCH - 1))
    g.sPE.inc(ins)
    g.lnsq_rd_pe = g.sPE.n
    R = g.smf
    dv_.wait_ge(g.sPE.h, g.sPE.n)
    dv_.tensor_scalar(R[0:1, 0:L], g.ps_bc[0:1, 0:L], 1.0 / D, None, ALU.mult)
    ins = dv_.tensor_scalar(R[0:1, 512:512 + L], g.ps_bc[0:1, 512:512 + L], 1.0 / D, EPS, ALU.mult, ALU.add)
    g.sDV.inc(ins)
    g.ps_bc_rd_dv = g.sDV.n
    dv_.wait_ge(g.sDV.h, g.sDV.n)
    g.sDV.inc(dv_.tensor_tensor(R[0:1, 1024:1024 + L], R[0:1, 0:L], R[0:1, 0:L], ALU.mult))
    dv_.wait_ge(g.sDV.h, g.sDV.n)
    g.sDV.inc(dv_.tensor_tensor(R[0:1, 1536:1536 + L], R[0:1, 512:512 + L], R[0:1, 1024:1024 + L], ALU.subtract))
    ac_.wait_ge(g.sDV.h, g.sDV.n)
    g.sAC.inc(ac_.activation(R[0:1, 1024:1024 + L], R[0:1, 1536:1536 + L], AF.Ln))
    ac_.wait_ge(g.sAC.h, g.sAC.n)
    g.sAC.inc(ac_.activation(g.lnrb[0:1, 0:L], R[0:1, 1024:1024 + L], AF.Exp, scale=-0.5))
    dv_.wait_ge(g.sAC.h, g.sAC.n)
    g.sDV.inc(dv_.tensor_tensor(g.lnrb[0:1, 512:512 + L], R[0:1, 0:L], g.lnrb[0:1, 0:L], ALU.mult))
    pe_.wait_ge(g.sDV.h, g.sDV.n)
    pe_.matmul(g.ps_bc[:, 0:L], g.ones[0:1, :], g.lnrb[0:1, 512:512 + L], start=True, stop=True)
    ins = pe_.matmul(g.ps_bc[:, 512:512 + L], g.ones[0:1, :], g.lnrb[0:1, 0:L], start=True, stop=True)
    g.sPE.inc(ins)
    dv_.wait_ge(g.sPE.h, g.sPE.n)
    for sm, mk in g.rb_rd:
        dv_.wait_ge(sm.h, mk)
    dv_.tensor_copy(g.Rb[:, 0:L], g.ps_bc[:, 512:512 + L])
    g.sDV.inc(dv_.tensor_copy(g.MRb[:, 0:L], g.ps_bc[:, 0:L]))
    g.ps_bc_rd_dv = g.sDV.n
    dv_.wait_ge(g.sDV.h, g.sDV.n)
    for c in range(NCH):
        ins = dv_.tensor_tensor(dst[c], dst[c], g.Rb[:, 0:L], ALU.mult)
    g.sDV.inc(ins)
    dv_.wait_ge(g.sDV.h, g.sDV.n)
    for c in range(NCH):
        ins = dv_.tensor_tensor(dst[c], dst[c], g.MRb[:, 0:L], ALU.subtract)
    g.sDV.inc(ins)
    g.rb_rd = [(g.sDV, g.sDV.n)]
    if residual is not None:
        dv_.wait_ge(g.sDV.h, g.sDV.n)
        for c in range(NCH):
            ins = dv_.tensor_tensor(residual[c], residual[c], dst[c], ALU.add)
        g.sDV.inc(ins)
    return g.sDV.n


def emit_l1_ffn(g, s):
    nc = g.nc
    scr = g.scr
    xn = [scr[:, 0:4096].rearrange("p (c t) -> p c t", c=NCH)[:, c, :] for c in range(NCH)]
    gu_slot = [scr[:, 4096 + 1024 * i: 4096 + 1024 * (i + 1)] for i in range(4)]
    wd_slot = [scr[:, 12288 + 1024 * 0: 12288 + 1024 * 0 + 0]]
    wd_slot = [scr[:, 8192 + 3072 * i: 8192 + 3072 * (i + 1)] for i in range(2)]
    h = g.bigb
    g.ln_sq = [g.biga[:, 512 * c:512 * (c + 1)] for c in range(NCH)]
    tmp_bf = [g.biga[:, 4096 + 512 * c: 4096 + 512 * (c + 1)] for c in range(NCH)]

    slot_mark = [0] * 4
    wdm = [0, 0]
    gate_bank_rd = [0] * 4
    up_bank_rd = [0] * 4
    down_bank_rd = [0, 0]
    sg_rd = [0, 0]
    acc_rd = 0
    xn_rd = 0
    h_rd = 0

    sy(g).wait_ge(g.sO.h, g.sO.n)

    for tq in range(NTC):
        tsl = slice(TC * tq, TC * (tq + 1))
        xin = [g.x[:, c, tsl] for c in range(NCH)]
        dv(g).wait_ge(g.sPE.h, xn_rd)
        ac(g).wait_ge(g.sPE.h, xn_rd)
        emit_ln(g, xin, xn, TC)
        xnm = g.sDV.n
        if s == 0 and tq == 0:
            dump_bf(g, g.scr[:, 0:4096], 4096)

        for oc in range(24):
            i0, i1 = (2 * oc) % 4, (2 * oc + 1) % 4
            sy_ = sy(g)
            sy_.wait_ge(g.sPE.h, slot_mark[i0])
            g.sR[i0].inc(sy_.dma_start(out=gu_slot[i0], in_=g.DW["wgT"][oc].rearrange("p k e -> p (k e)")), 16)
            sy_.wait_ge(g.sPE.h, slot_mark[i1])
            g.sR[i1].inc(sy_.dma_start(out=gu_slot[i1], in_=g.DW["wuT"][oc].rearrange("p k e -> p (k e)")), 16)
            wm0, wm1 = g.sR[i0].n, g.sR[i1].n
            b0, b1 = (2 * oc) % 4, (2 * oc + 1) % 4
            psg, psu_ = g.pp[b0], g.pp[b1]
            pe_ = pe(g)
            pe_.wait_ge(g.sR[i0].h, wm0)
            pe_.wait_ge(g.sR[i1].h, wm1)
            pe_.wait_ge(g.sDV.h, xnm)
            pe_.wait_ge(g.sAC.h, gate_bank_rd[b0])
            pe_.wait_ge(g.sDV.h, up_bank_rd[b1])
            for k in range(NCH):
                ins = pe_.matmul(psg, gu_slot[i0][:, 128 * k:128 * (k + 1)], xn[k],
                                 start=(k == 0), stop=(k == NCH - 1))
            g.sPE.inc(ins)
            gm = g.sPE.n
            for k in range(NCH):
                ins = pe_.matmul(psu_, gu_slot[i1][:, 128 * k:128 * (k + 1)], xn[k],
                                 start=(k == 0), stop=(k == NCH - 1))
            g.sPE.inc(ins)
            slot_mark[i0] = slot_mark[i1] = xn_rd = g.sPE.n
            ac_ = ac(g)
            ac_.wait_ge(g.sPE.h, gm)
            ac_.wait_ge(g.sDV.h, sg_rd[oc % 2])
            g.sAC.inc(ac_.activation(g.sg_t[oc % 2][:], psg, AF.Silu, scale=QS))
            gate_bank_rd[b0] = g.sAC.n
            dv_ = dv(g)
            dv_.wait_ge(g.sAC.h, g.sAC.n)
            dv_.wait_ge(g.sPE.h, g.sPE.n)
            dv_.wait_ge(g.sPE.h, h_rd)
            ins = dv_.tensor_tensor(h[:, 512 * oc:512 * (oc + 1)], g.sg_t[oc % 2][:], psu_, ALU.mult)
            g.sDV.inc(ins)
            up_bank_rd[b1] = sg_rd[oc % 2] = g.sDV.n
            if s == 0 and tq == 0 and oc == 0:
                dump_bf(g, g.sg_t[0][:], 512)
                dump_bf(g, h[:, 0:512], 512)
        hm = g.sDV.n

        for po in range(8):
            sl = wd_slot[po % 2]
            sy_ = sy(g)
            sy_.wait_ge(g.sPE.h, wdm[po % 2])
            g.sR[8 + po % 2].inc(sy_.dma_start(out=sl, in_=g.DW["wdT"][po].rearrange("p k e -> p (k e)")), 16)
            wm = g.sR[8 + po % 2].n
            psd = g.pp[4 + po % 2]
            pe_ = pe(g)
            pe_.wait_ge(g.sR[8 + po % 2].h, wm)
            pe_.wait_ge(g.sDV.h, hm)
            pe_.wait_ge(g.sAC.h, down_bank_rd[po % 2])
            for k in range(24):
                ins = pe_.matmul(psd, sl[:, 128 * k:128 * (k + 1)], h[:, 512 * k:512 * (k + 1)],
                                 start=(k == 0), stop=(k == 23))
            g.sPE.inc(ins)
            wdm[po % 2] = h_rd = g.sPE.n
            ac_ = ac(g)
            ac_.wait_ge(g.sPE.h, g.sPE.n)
            ac_.wait_ge(g.sDV.h, acc_rd)
            g.sAC.inc(ac_.activation(g.accb[:, po, :], psd, AF.Copy, scale=QS * QS))
            down_bank_rd[po % 2] = g.sAC.n
            if s == 0 and tq == 0 and po == 0:
                dump_f(g, g.accb[:, 0, :], 512)
                sy(g).wait_ge(g.sR[8].h, wm)
                dump_bf(g, wd_slot[0], 3072)
                dump_bf(g, h[:, 0:4096], 4096)

        accin = [g.accb[:, c, :] for c in range(NCH)]
        dv_ = dv(g)
        dv_.wait_ge(g.sAC.h, g.sAC.n)
        emit_ln(g, accin, None, TC, tmp_bf=tmp_bf, residual=xin)
        acc_rd = g.sDV.n


def emit_l0_attn(g, s):
    scr = g.scr
    xn = [scr[:, 0:4096].rearrange("p (c t) -> p c t", c=NCH)[:, c, :] for c in range(NCH)]
    kT = scr[:, 0:1024]
    vT = scr[:, 1024:2048]
    k_t = [scr[:, 4096 + 512 * c: 4096 + 512 * (c + 1)] for c in range(NCH)]
    v_t = [scr[:, 8192 + 512 * c: 8192 + 512 * (c + 1)] for c in range(NCH)]
    pin = [scr[:, 0:4096].rearrange("p (c t) -> p c t", c=NCH)[:, c, :] for c in range(NCH)]
    ring = [scr[:, 12288 + 1024 * i: 12288 + 1024 * (i + 1)] for i in range(2)]
    g.ln_sq = [g.smb[:, 2048 + 128 * c: 2048 + 128 * (c + 1)] for c in range(NCH)]
    ctx_bf = g.ctxf[:].bitcast(BF16)
    tmp_bf = [ctx_bf[:, 128 * c: 128 * (c + 1)] for c in range(NCH)]
    vk_sb = g.smb[:, 1024:1280]
    ks_bd = g.smb[:, 1280:1536]
    rden_b = g.smb[:, 1536:2048]
    denq = g.smb[:, 4352:4864]
    den_sb = g.smb[:, 4352:4864]
    psA, psB, psC = g.psA, g.psB, g.psC
    psC_bf = psC[:].bitcast(BF16)

    g.vk_acc = g.smf[:, 3072:4096]
    g.ks_acc = g.smf[:, 2561:2569]
    g.vkacc_dv = 0
    slot_pe = [0, 0]
    qkv_ev = [0, 0]
    kv_rd = 0
    kvT_rd = 0
    kvT_ev = 0
    xn_rd = 0
    started = {}

    def mmacc(pe_, bank_key, out, lhsT, rhs, tp=None):
        st = bank_key not in started
        started[bank_key] = True
        return pe_.matmul(out, lhsT, rhs, start=st, stop=False, skip_group_check=True,
                          tile_position=tp)

    pe_, dv_, ac_, pl_ = pe(g), dv(g), ac(g), pl(g)
    sy(g).wait_ge(g.sO.h, g.prep_job_o["d_qkvT"])
    ac_.wait_ge(g.sDV.h, g.prep_done_dv)
    pe_.wait_ge(g.sX.h, g.sX.n)

    for tc in range(NTC):
        tsl = slice(TC * tc, TC * (tc + 1))
        dv(g).wait_ge(g.sPE.h, xn_rd)
        ac(g).wait_ge(g.sPE.h, xn_rd)
        for j4 in range(4):
            sl = slice(TC * tc + 128 * j4, TC * tc + 128 * (j4 + 1))
            xin4 = [g.x[:, c, sl] for c in range(NCH)]
            xnsl = [xn[c][:, 128 * j4:128 * (j4 + 1)] for c in range(NCH)]
            emit_ln(g, xin4, xnsl, 128)
        xnm = g.sDV.n
        for oc in range(24):
            sl_i = oc % 2
            sy_ = sy(g)
            sy_.wait_ge(g.sPE.h, slot_pe[sl_i])
            g.sR[sl_i].inc(sy_.dma_start(out=ring[sl_i], in_=g.DW["qkvT"][oc].rearrange("p k e -> p (k e)")), 16)
            wv = g.sR[sl_i].n
            psq = psB if oc % 2 == 0 else psC
            pe_.wait_ge(g.sR[sl_i].h, wv)
            pe_.wait_ge(g.sDV.h, xnm)
            pe_.wait_ge(g.sAC.h, qkv_ev[oc % 2])
            if oc < 2:
                pe_.wait_ge(g.sDV.h, kvT_ev)
            for k in range(NCH):
                ins = pe_.matmul(psq[:], ring[sl_i][:, 128 * k:128 * (k + 1)], xn[k],
                                 start=(k == 0), stop=(k == NCH - 1))
            g.sPE.inc(ins)
            slot_pe[sl_i] = g.sPE.n
            ac_.wait_ge(g.sPE.h, g.sPE.n)
            if oc < 8:
                dst = g.biga[:, 2048 * oc + TC * tc: 2048 * oc + TC * (tc + 1)]
            elif oc < 16:
                ac_.wait_ge(g.sPE.h, kv_rd)
                dst = k_t[oc - 8]
            else:
                ac_.wait_ge(g.sPE.h, kv_rd)
                dst = v_t[oc - 16]
            g.sAC.inc(ac_.activation(dst, psq[:], AF.Relu))
            qkv_ev[oc % 2] = g.sAC.n
        qkvm = g.sAC.n
        if s == 0 and tc == 0:
            dump_bf(g, g.scr[:, 0:4096], 4096)
            dump_bf(g, g.biga[:, 0:512], 512)
            dump_bf(g, k_t[0], 512)
            dump_bf(g, v_t[0], 512)
        for t4 in range(4):
            pe_.wait_ge(g.sAC.h, qkvm)
            pe_.wait_ge(g.sDV.h, kvT_ev)
            for c in range(NCH):
                ins = pe_.transpose(psC_bf[:, 128 * c:128 * (c + 1)],
                                    k_t[c][:, 128 * t4:128 * (t4 + 1)], g.ident[:])
            g.sPE.inc(ins)
            dv_.wait_ge(g.sPE.h, g.sPE.n)
            dv_.wait_ge(g.sPE.h, kvT_rd)
            g.sDV.inc(dv_.tensor_copy(kT[:], psC_bf[:, 0:1024]))
            kvT_ev = g.sDV.n
            pe_.wait_ge(g.sDV.h, kvT_ev)
            for c in range(NCH):
                ins = pe_.transpose(psC_bf[:, 128 * c:128 * (c + 1)],
                                    v_t[c][:, 128 * t4:128 * (t4 + 1)], g.ident[:])
            g.sPE.inc(ins)
            kv_rd = g.sPE.n
            dv_.wait_ge(g.sPE.h, g.sPE.n)
            g.sDV.inc(dv_.tensor_copy(vT[:], psC_bf[:, 0:1024]))
            kvT_ev = g.sDV.n
            pe_.wait_ge(g.sDV.h, kvT_ev)
            pe_.wait_ge(g.sDV.h, g.vkacc_dv)
            for hg in range(8):
                for j in range(4):
                    hcol = 128 * hg + 32 * j
                    ins = pe_.matmul(psA[32 * j:32 * (j + 1), hcol:hcol + 32],
                                     kT[:, hcol:hcol + 32], vT[:, hcol:hcol + 32],
                                     start=True, stop=True, skip_group_check=True,
                                     tile_position=(0, 32 * j))
            for hg in range(8):
                for j in range(4):
                    hcol = 128 * hg + 32 * j
                    ins = pe_.matmul(psA[32 * j:32 * (j + 1), 1024 + hg:1024 + hg + 1],
                                     kT[:, hcol:hcol + 32], g.ones[:, 0:1],
                                     start=True, stop=True, skip_group_check=True,
                                     tile_position=(0, 32 * j))
            g.sPE.inc(ins)
            kvT_rd = g.sPE.n
            dv_.wait_ge(g.sPE.h, g.sPE.n)
            first = (tc == 0 and t4 == 0)
            op = ALU.bypass if first else ALU.add
            if first:
                dv_.tensor_copy(g.vk_acc[:], psA[:, 0:1024])
                ins2 = dv_.tensor_copy(g.ks_acc[:], psA[:, 1024:1032])
            else:
                dv_.tensor_tensor(g.vk_acc[:], g.vk_acc[:], psA[:, 0:1024], ALU.add)
                ins2 = dv_.tensor_tensor(g.ks_acc[:], g.ks_acc[:], psA[:, 1024:1032], ALU.add)
            g.sDV.inc(ins2)
            g.vkacc_dv = g.sDV.n
        xn_rd = g.sPE.n

    vkm = g.sPE.n
    dv_.wait_ge(g.sPE.h, vkm)
    g.sDV.inc(dv_.memset(ks_bd[:], 0.0))
    dv_.wait_ge(g.sDV.h, g.sDV.n)
    for hg in range(8):
        for j in range(4):
            hcol = 128 * hg + 32 * j
            dv_.tensor_copy(vk_sb[32 * j:32 * (j + 1), 32 * hg:32 * (hg + 1)],
                            g.vk_acc[32 * j:32 * (j + 1), hcol:hcol + 32])
            ins = dv_.tensor_copy(ks_bd[32 * j:32 * (j + 1), 32 * hg + 4 * hg + j:32 * hg + 4 * hg + j + 1],
                                  g.ks_acc[32 * j:32 * (j + 1), hg:hg + 1])
    g.sDV.inc(ins)
    evm = g.sDV.n
    if s == 0:
        dump_bf(g, vk_sb[:], 256)
        dump_bf(g, ks_bd[:], 256)
    g.sDV.inc(dv_.memset(g.smf[:, 2560:2561], 1e-15))
    eps_col = g.smf[:, 2560:2561]
    pe_.wait_ge(g.sDV.h, evm)
    for tc in range(NTC):
        for hg in range(8):
            ins = pe_.matmul(psA[0:32, 512 * tc:512 * (tc + 1)], ks_bd[:, 32 * hg:32 * (hg + 1)],
                             g.biga[:, 2048 * hg + TC * tc: 2048 * hg + TC * (tc + 1)],
                             start=(hg == 0), stop=(hg == 7))
    g.sPE.inc(ins)
    ac_.wait_ge(g.sPE.h, g.sPE.n)
    ac_.wait_ge(g.sDV.h, g.sDV.n)
    for tc4 in range(NTC):
        g.sAC.inc(ac_.activation(g.ctxf[0:32, 0:512], psA[0:32, 512 * tc4:512 * (tc4 + 1)],
                                 AF.Ln, bias=eps_col[0:32, :]))
        ac_.wait_ge(g.sAC.h, g.sAC.n)
        g.sAC.inc(ac_.activation(g.bigb[0:32, 512 * tc4:512 * (tc4 + 1)], g.ctxf[0:32, 0:512],
                                 AF.Exp, scale=-1.0))
        ac_.wait_ge(g.sAC.h, g.sAC.n)
    dv_.wait_ge(g.sAC.h, g.sAC.n)
    g.sDV.inc(dv_.tensor_copy(g.bigb[32:64, 0:2048], g.bigb[0:32, 0:2048]))
    sy_ = sy(g)
    sy_.wait_ge(g.sDV.h, g.sDV.n)
    for tc4 in range(NTC):
        g.sR[11].inc(sy_.dma_start(out=g.DW["den"][:, 512 * tc4:512 * (tc4 + 1)],
                                   in_=g.bigb[32:64, 512 * tc4:512 * (tc4 + 1)]), 16)
    den_st = g.sR[11].n

    if s == 0:
        sy(g).wait_ge(g.sAC.h, g.sAC.n)
        dump_bf(g, g.bigb[0:32, 0:2048], 2048, rows=32)
    projm = 0
    numtt = g.sDV.n
    nq_pe = 0
    accq = 0
    dv_.wait_ge(g.sR[11].h, den_st)
    for tc in range(NTC):
        tsl = slice(TC * tc, TC * (tc + 1))
        for hg in range(8):
            sy_ = sy(g)
            sy_.wait_ge(g.sR[11].h, den_st)
            sy_.wait_ge(g.sPE.h, nq_pe)
            g.sR[10].inc(sy_.dma_start(out=denq[0:4, :], in_=g.DW["den"][4 * hg:4 * (hg + 1), tsl]), 16)
            if s == 0 and tc == 1 and hg == 0:
                sy(g).wait_ge(g.sR[10].h, g.sR[10].n)
                dump_bf(g, denq[0:4, :], 512, rows=4)
                dump_bf(g, g.DW["den"][:, 512:1024], 512, rows=32)
            pe_.wait_ge(g.sDV.h, numtt)
            pe_.wait_ge(g.sAC.h, qkv_ev[0])
            for j in range(4):
                ins = pe_.matmul(psB[32 * j:32 * (j + 1), :], vk_sb[32 * j:32 * (j + 1), 32 * hg:32 * (hg + 1)],
                                 g.biga[32 * j:32 * (j + 1), 2048 * hg + TC * tc: 2048 * hg + TC * (tc + 1)],
                                 start=True, stop=True, skip_group_check=True,
                                 tile_position=(32 * j, 32 * j))
            g.sPE.inc(ins)
            numm = g.sPE.n
            pe_.wait_ge(g.sR[10].h, g.sR[10].n)
            pe_.wait_ge(g.sDV.h, kvT_ev)
            pe_.wait_ge(g.sAC.h, qkv_ev[1])
            for j in range(4):
                ins = pe_.matmul(psC[32 * j:32 * (j + 1), :], g.sel[:, 128 * j:128 * j + 32],
                                 denq[0:4, :], start=True, stop=True, skip_group_check=True,
                                 tile_position=(0, 32 * j))
            g.sPE.inc(ins)
            nq_pe = g.sPE.n
            dv_.wait_ge(g.sPE.h, nq_pe)
            g.sDV.inc(dv_.tensor_copy(rden_b[:], psC[:]))
            kvT_ev = g.sDV.n
            dv_.wait_ge(g.sDV.h, g.sDV.n)
            dv_.wait_ge(g.sPE.h, projm)
            g.sDV.inc(dv_.tensor_tensor(pin[hg], rden_b[:], psB[:], ALU.mult))
            numtt = g.sDV.n
            if s == 0 and tc == 1 and hg == 0:
                dump_bf(g, rden_b[:], 512)
                dump_bf(g, pin[hg], 512)
        pinm = g.sDV.n
        if s == 0 and tc == 1:
            dump_bf(g, g.scr[:, 0:4096], 4096)
        for po in range(8):
            sl_i = po % 2
            sy_ = sy(g)
            sy_.wait_ge(g.sO.h, g.prep_job_o["d_projT"])
            sy_.wait_ge(g.sPE.h, slot_pe[sl_i])
            g.sR[sl_i].inc(sy_.dma_start(out=ring[sl_i], in_=g.DW["projT"][po].rearrange("p k e -> p (k e)")), 16)
            wv = g.sR[sl_i].n
            psq = psB if po % 2 == 0 else psC
            pe_.wait_ge(g.sR[sl_i].h, wv)
            pe_.wait_ge(g.sDV.h, pinm)
            pe_.wait_ge(g.sAC.h, qkv_ev[po % 2])
            for k in range(NCH):
                ins = pe_.matmul(psq[:], ring[sl_i][:, 128 * k:128 * (k + 1)], pin[k],
                                 start=(k == 0), stop=(k == NCH - 1))
            g.sPE.inc(ins)
            slot_pe[sl_i] = g.sPE.n
            projm = g.sPE.n
            ac_.wait_ge(g.sPE.h, g.sPE.n)
            ac_.wait_ge(g.sDV.h, accq)
            g.sAC.inc(ac_.activation(g.accb[:, po, :], psq[:], AF.Identity,
                                     bias=g.bias_cols[:, po:po + 1]))
            qkv_ev[po % 2] = g.sAC.n
        if s == 0 and tc == 1:
            sy(g).wait_ge(g.sAC.h, g.sAC.n)
            dump_f(g, g.accb[:, 0, :], 512)
        for j4 in range(4):
            sl = slice(TC * tc + 128 * j4, TC * tc + 128 * (j4 + 1))
            xin = [g.x[:, c, sl] for c in range(NCH)]
            accin = [g.accb[:, c, 128 * j4:128 * (j4 + 1)] for c in range(NCH)]
            dv_.wait_ge(g.sAC.h, g.sAC.n)
            emit_ln(g, accin, None, 128, tmp_bf=tmp_bf, residual=xin)
        accq = g.sDV.n


def emit_l1_attn(g, s):
    scr = g.scr
    xn = [scr[:, 0:4096].rearrange("p (c t) -> p c t", c=NCH)[:, c, :] for c in range(NCH)]
    wts = scr[:, 0:4096]
    qsl = lambda gp: scr[:, 4096 + 512 * gp: 4096 + 512 * (gp + 1)]
    ring = [scr[:, 8192 + 2048 * i: 8192 + 2048 * (i + 1)] for i in range(2)]
    wbf = scr[:, 12288:14336]
    vtmp = [g.smb[:, 1024:1536], g.smb[:, 1536:2048]]
    g.ln_sq = [g.smb[:, 2048 + 128 * c: 2048 + 128 * (c + 1)] for c in range(NCH)]
    ctx_bf = g.ctxf[:].bitcast(BF16)
    tmp_bf = [ctx_bf[:, 128 * c: 128 * (c + 1)] for c in range(NCH)]
    ctxwb = ctx_bf
    negmax = g.smf[:, 3072:3074]
    ssum = g.smf[:, 3074:3076]
    rsum = g.smf[:, 3076:3078]
    psA, psB, psC = g.psA, g.psB, g.psC
    psA_bf = psA[:].bitcast(BF16)
    psC_bf = psC[:].bitcast(BF16)

    slot_pe = [0, 0]
    vt_ev = 0
    vtmp_rd = [0, 0]
    xn_rd = 0
    g.l1a_wts_rd = [0, 0]
    g.l1a_psC_rd = (g.sDV, 0)
    g.psA_rd = [[], []]
    g.l1a_was_rot = True
    g.l1a_wb_rd = [0, 0]
    g.l1a_ctx_ev = 0
    g.l1a_proj_ev = 0
    acc_rd = 0

    for tc in range(NTC):
        tsl = slice(TC * tc, TC * (tc + 1))
        dv(g).wait_ge(g.sPE.h, xn_rd)
        ac(g).wait_ge(g.sPE.h, xn_rd)
        for j4 in range(4):
            sl = slice(TC * tc + 128 * j4, TC * tc + 128 * (j4 + 1))
            xin = [g.x[:, c, sl] for c in range(NCH)]
            xnsl = [xn[c][:, 128 * j4:128 * (j4 + 1)] for c in range(NCH)]
            emit_ln(g, xin, xnsl, 128)
        xnm = g.sDV.n

        pe_, dv_, ac_, pl_ = pe(g), dv(g), ac(g), pl(g)
        for gp in range(8):
            for mi, (dwn, bcol) in enumerate((("aq", None), ("ak", 16 + gp), ("av", 24 + gp))):
                sl_i = (3 * gp + mi) % 2
                sy_ = sy(g)
                sy_.wait_ge(g.sPE.h, slot_pe[sl_i])
                g.sR[sl_i].inc(sy_.dma_start(out=ring[sl_i][:, 0:1024], in_=g.DW[dwn][gp].rearrange("p k e -> p (k e)")), 16)
                wv = g.sR[sl_i].n
                pe_.wait_ge(g.sR[sl_i].h, wv)
                pe_.wait_ge(g.sDV.h, xnm)
                pe_.wait_ge(g.sAC.h, g.l1a_proj_ev)
                pe_.wait_ge(g.sDV.h, g.l1a_ctx_ev)
                for k in range(NCH):
                    ins = pe_.matmul(psB[:], ring[sl_i][:, 128 * k:128 * (k + 1)], xn[k],
                                     start=(k == 0), stop=(k == NCH - 1))
                g.sPE.inc(ins)
                slot_pe[sl_i] = g.sPE.n
                ac_.wait_ge(g.sPE.h, g.sPE.n)
                if mi == 0:
                    g.sAC.inc(ac_.activation(qsl(gp), psB[:], AF.Identity, scale=0.125,
                                             bias=g.bias_colsq[:, gp:gp + 1]))
                elif mi == 1:
                    g.sAC.inc(ac_.activation(g.biga[:, 2048 * gp + TC * tc: 2048 * gp + TC * (tc + 1)],
                                             psB[:], AF.Identity, bias=g.bias_cols[:, bcol:bcol + 1]))
                else:
                    vslot = gp % 2
                    ac_.wait_ge(g.sPE.h, vtmp_rd[vslot])
                    g.sAC.inc(ac_.activation(vtmp[vslot][:], psB[:], AF.Identity,
                                             bias=g.bias_cols[:, bcol:bcol + 1]))
                g.l1a_proj_ev = g.sAC.n
            vslot = gp % 2
            pe_.wait_ge(g.sAC.h, g.l1a_proj_ev)
            pe_.wait_ge(g.sDV.h, vt_ev)
            for t4 in range(4):
                ins = pe_.transpose(psC_bf[:, 128 * t4:128 * (t4 + 1)],
                                    vtmp[vslot][:, 128 * t4:128 * (t4 + 1)], g.ident[:])
            g.sPE.inc(ins)
            vtmp_rd[vslot] = g.sPE.n
            dv_.wait_ge(g.sPE.h, g.sPE.n)
            for t4 in range(4):
                ktc = 4 * tc + t4
                ins = dv_.tensor_copy(g.bigb[:, 1024 * ktc + 128 * gp: 1024 * ktc + 128 * (gp + 1)],
                                      psC_bf[:, 128 * t4:128 * (t4 + 1)])
            g.sDV.inc(ins)
            vt_ev = g.sDV.n

        if s == 0 and tc == 0:
            dump_bf(g, g.scr[:, 0:4096], 4096)
            dump_bf(g, qsl(0), 512)
            dump_bf(g, g.biga[:, 0:512], 512)
            dump_bf(g, g.bigb[:, 0:1024], 1024)
        for qi in range(4):
            qig = 4 * tc + qi
            nkt = 128 * (qig + 1)
            rot = nkt <= 1024

            def finish_chain(st):
                h2, reg2, wb2, expm = st
                dv_.wait_ge(g.sAC.h, expm)
                g.sDV.inc(dv_.reciprocal(rsum[:, reg2:reg2 + 1], ssum[:, reg2:reg2 + 1]))
                dv_.wait_ge(g.sDV.h, g.sDV.n)
                g.sDV.inc(dv_.tensor_scalar(wb2, wb2, rsum[:, reg2:reg2 + 1], None, ALU.mult))

            def tail_a(st):
                h2, reg2, wb2, expm = st
                wp = h2 % 2
                wdst = wts[:, 2048 * wp:2048 * wp + nkt]
                ntile = qig + 1
                wtsem = None
                pe_.wait_ge(g.sDV.h, g.sDV.n)
                pe_.wait_ge(g.sDV.h, vt_ev)
                for b in range(0, ntile, 8):
                    nb = min(8, ntile - b)
                    sm, mk = g.l1a_psC_rd
                    pe_.wait_ge(sm.h, mk)
                    for j5 in range(nb):
                        ins = pe_.transpose(psC_bf[:, 128 * j5:128 * (j5 + 1)],
                                            wb2[:, 128 * (b + j5):128 * (b + j5 + 1)], g.ident[:])
                    g.sPE.inc(ins)
                    if h2 % 2 == 0:
                        dv_.wait_ge(g.sPE.h, g.sPE.n)
                        dv_.wait_ge(g.sPE.h, g.l1a_wts_rd[wp])
                        g.sDV.inc(dv_.tensor_copy(wdst[:, 1024 * (b // 8):1024 * (b // 8) + 128 * nb],
                                                  psC_bf[:, 0:128 * nb]))
                        wtsem = (g.sDV, g.sDV.n)
                    else:
                        ac_.wait_ge(g.sPE.h, g.sPE.n)
                        ac_.wait_ge(g.sPE.h, g.l1a_wts_rd[wp])
                        g.sAC.inc(ac_.activation(wdst[:, 1024 * (b // 8):1024 * (b // 8) + 128 * nb],
                                                 psC_bf[:, 0:128 * nb], AF.Copy))
                        wtsem = (g.sAC, g.sAC.n)
                    g.l1a_psC_rd = wtsem
                g.l1a_wb_rd[h2 % 2] = g.sPE.n
                return (h2, wtsem)

            def tail_b(stb):
                h2, wtsem = stb
                gp2, sub2 = h2 // 2, h2 % 2
                wp = h2 % 2
                pe_.wait_ge(wtsem[0].h, wtsem[1])
                if sub2 == 0:
                    pe_.wait_ge(g.sDV.h, g.l1a_ctx_ev)
                    pe_.wait_ge(g.sAC.h, g.l1a_proj_ev)
                for kt in range(qig + 1):
                    ins = pe_.matmul(psB[64 * sub2:64 * (sub2 + 1), 0:128],
                                     g.bigb[:, 1024 * kt + 128 * gp2 + 64 * sub2: 1024 * kt + 128 * gp2 + 64 * (sub2 + 1)],
                                     wts[:, 2048 * wp + 128 * kt:2048 * wp + 128 * (kt + 1)],
                                     start=(kt == 0), stop=(kt == qig))
                g.sPE.inc(ins)
                g.l1a_wts_rd[wp] = g.sPE.n
                if sub2 == 1:
                    dv_.wait_ge(g.sPE.h, g.sPE.n)
                    g.sDV.inc(dv_.tensor_copy(g.accb[:, gp2, 128 * qi:128 * (qi + 1)], psB[:, 0:128]))
                    g.l1a_ctx_ev = g.sDV.n

            def flush(pa, pb):
                if pb is not None:
                    tail_b(pb)
                if pa is not None:
                    finish_chain(pa)
                    tail_b(tail_a(pa))
                return None, None

            pend_a = None
            pend_b = None
            for h in range(16):
                gp, sub = h // 2, h % 2
                reg = (h % 2) if rot else 0
                ro = 1024 * reg
                if rot != g.l1a_was_rot:
                    pend_a, pend_b = flush(pend_a, pend_b)
                    for sm, mk in g.psA_rd[0] + g.psA_rd[1]:
                        pe_.wait_ge(sm.h, mk)
                    g.psA_rd = [[], []]
                    g.l1a_was_rot = rot
                for sm, mk in g.psA_rd[reg]:
                    pe_.wait_ge(sm.h, mk)
                g.psA_rd[reg] = []
                wbp = h % 2
                if rot:
                    wb = wbf[:, 1024 * wbp:1024 * wbp + nkt]
                else:
                    wb = wbf[:, 0:nkt] if wbp == 0 else ctxwb[:, 0:nkt]
                q_ap = qsl(gp)[64 * sub:64 * (sub + 1), 128 * qi:128 * (qi + 1)]
                nm2 = (nkt + 511) // 512
                for kb in range(nm2):
                    n0 = 512 * kb
                    nn = min(512, nkt - n0)
                    ins = pe_.matmul(psA[:, ro + n0: ro + n0 + nn], q_ap,
                                     g.biga[64 * sub:64 * (sub + 1), 2048 * gp + n0: 2048 * gp + n0 + nn],
                                     start=True, stop=True)
                g.sPE.inc(ins)
                scm = g.sPE.n
                dv_.wait_ge(g.sPE.h, scm)
                g.sDV.inc(dv_.tensor_tensor(psA[:, ro + nkt - 128: ro + nkt],
                                            psA[:, ro + nkt - 128: ro + nkt], g.cmask[:], ALU.add))
                dv_.wait_ge(g.sDV.h, g.sDV.n)
                g.sDV.inc(dv_.reduce_max(negmax[:, reg:reg + 1], psA[:, ro: ro + nkt], axis=AX.X, negate=True))
                ac_.wait_ge(g.sDV.h, g.sDV.n)
                ac_.wait_ge(g.sPE.h, g.l1a_wb_rd[wbp])
                g.sAC.inc(ac_.activation(wb, psA[:, ro: ro + nkt], AF.Exp,
                                         bias=negmax[:, reg:reg + 1], accum_out=ssum[:, reg:reg + 1]))
                g.psA_rd[reg].append((g.sAC, g.sAC.n))
                st = (h, reg, wb, g.sAC.n)
                nb = None
                if pend_a is not None:
                    finish_chain(pend_a)
                    nb = tail_a(pend_a)
                if pend_b is not None:
                    tail_b(pend_b)
                pend_a, pend_b = st, nb
            pend_a, pend_b = flush(pend_a, pend_b)
            sl = slice(TC * tc + 128 * qi, TC * tc + 128 * (qi + 1))
            xin = [g.x[:, c, sl] for c in range(NCH)]
            accin = [g.accb[:, c, 128 * qi:128 * (qi + 1)] for c in range(NCH)]
            dv_.wait_ge(g.sDV.h, g.l1a_ctx_ev)
            emit_ln(g, accin, None, 128, tmp_bf=tmp_bf, residual=xin)
            acc_rd = g.sDV.n
        xn_rd = g.sPE.n


def emit_l0_ffn(g, s):
    scr = g.scr
    xn = [scr[:, 0:4096].rearrange("p (c t) -> p c t", c=NCH)[:, c, :] for c in range(NCH)]
    hsl = lambda hu, k2: scr[:, 4096 + 1024 * hu + 512 * k2: 4096 + 1024 * hu + 512 * (k2 + 1)]
    cbc = [scr[:, 8192 + 512 * j: 8192 + 512 * (j + 1)] for j in range(2)]
    rs_ = [g.bigb[:, 2048 * i:2048 * (i + 1)] for i in range(8)]
    wr_sb = [g.smb[:, 4352 + 16 * k: 4352 + 16 * (k + 1)] for k in range(NCH)]
    cstg = g.smb[:, 4480:4992]
    C = g.ctxf
    e_t, top8, mt = C[:, 0:16], C[:, 16:24], C[:, 24:32]
    mr2, m12, cwf = C[:, 32:48], C[:, 48:64], C[:, 64:80]
    negmax, ssum, rsum = C[:, 80:81], C[:, 81:82], C[:, 82:83]
    g.ln_sq = [g.biga[:, 512 * c:512 * (c + 1)] for c in range(NCH)]
    tmp_bf = [g.biga[:, 4096 + 512 * c: 4096 + 512 * (c + 1)] for c in range(NCH)]
    pp = g.pp
    ps_bf16 = pp[5].bitcast(BF16)

    g.sR[12].inc(sy(g).dma_start(out=g.smb[:, 4352:4480], in_=g.DW["wr"].rearrange("p k e -> p (k e)")), 16)
    wrm = g.sR[12].n

    slot_pe = [0] * 8
    gate_rd = [0, 0]
    up_rd = [0, 0]
    sg_rd = [0, 0]
    down_ev = [(g.sAC, 0), (g.sAC, 0)]
    pp5_rd = 0
    acc_rd = 0
    xn_rd = 0
    h_rd = 0
    cwq_pe = 0
    cbc_rd = 0
    cstg_rd = 0

    for tq in range(NTC):
        tsl = slice(TC * tq, TC * (tq + 1))
        xin = [g.x[:, c, tsl] for c in range(NCH)]
        dv(g).wait_ge(g.sPE.h, xn_rd)
        ac(g).wait_ge(g.sPE.h, xn_rd)
        emit_ln(g, xin, xn, TC)
        xnm = g.sDV.n

        pe_, dv_, ac_, pl_ = pe(g), dv(g), ac(g), pl(g)
        pe_.wait_ge(g.sDV.h, g.ps_bc_rd_dv)
        pe_.wait_ge(g.sDV.h, xnm)
        pe_.wait_ge(g.sR[12].h, wrm)
        for j in range(4):
            for k in range(NCH):
                ins = pe_.matmul(g.ps_bc[:, 16 * j:16 * (j + 1)],
                                 xn[k][:, 128 * j:128 * (j + 1)], wr_sb[k],
                                 start=(k == 0), stop=(k == NCH - 1))
        g.sPE.inc(ins)
        rmark = g.sPE.n
        for j in range(4):
            lg = g.ps_bc[:, 16 * j:16 * (j + 1)]
            dv_.wait_ge(g.sPE.h, rmark)
            g.sDV.inc(dv_.reduce_max(negmax, lg, axis=AX.X, negate=True))
            ac_.wait_ge(g.sDV.h, g.sDV.n)
            g.sAC.inc(ac_.activation(e_t, lg, AF.Exp, bias=negmax))
            dv_.wait_ge(g.sAC.h, g.sAC.n)
            g.sDV.inc(dv_.reduce_sum(ssum, e_t, axis=AX.X))
            dv_.wait_ge(g.sDV.h, g.sDV.n)
            g.sDV.inc(dv_.reciprocal(rsum, ssum))
            dv_.wait_ge(g.sDV.h, g.sDV.n)
            g.sDV.inc(dv_.max(top8, e_t))
            dv_.wait_ge(g.sDV.h, g.sDV.n)
            g.sDV.inc(dv_.memset(mt, -1.0))
            dv_.wait_ge(g.sDV.h, g.sDV.n)
            g.sDV.inc(dv_.tensor_copy(mt[:, 0:2], top8[:, 0:2]))
            dv_.wait_ge(g.sDV.h, g.sDV.n)
            g.sDV.inc(dv_.match_replace(mr2, mt, e_t, -7.0))
            dv_.wait_ge(g.sDV.h, g.sDV.n)
            g.sDV.inc(dv_.tensor_single_scalar(m12, mr2, -7.0, ALU.is_equal))
            dv_.wait_ge(g.sDV.h, g.sDV.n)
            g.sDV.inc(dv_.tensor_mul(cwf, m12, e_t))
            dv_.wait_ge(g.sDV.h, g.sDV.n)
            g.sDV.inc(dv_.tensor_scalar(g.rt[:, 16 * j:16 * (j + 1)], cwf, rsum, None, ALU.mult))
        g.ps_bc_rd_dv = g.sDV.n
        rchain = g.sDV.n
        pe_.wait_ge(g.sDV.h, rchain)
        sm, vv = down_ev[1]
        pe_.wait_ge(sm.h, vv)
        pe_.wait_ge(g.sDV.h, pp5_rd)
        for j in range(4):
            ins = pe_.transpose(ps_bf16[0:16, 128 * j:128 * (j + 1)], g.rt[:, 16 * j:16 * (j + 1)], g.ident[:])
        g.sPE.inc(ins)
        dv_.wait_ge(g.sPE.h, g.sPE.n)
        dv_.wait_ge(g.sR[11].h, cstg_rd)
        g.sDV.inc(dv_.tensor_copy(cstg[0:16, :], ps_bf16[0:16, 0:512]))
        pp5_rd = g.sDV.n
        sy_ = sy(g)
        sy_.wait_ge(g.sDV.h, g.sDV.n)
        g.sR[11].inc(sy_.dma_start(out=g.DW["cw"][:, tsl], in_=cstg[0:16, :]), 16)
        cw_st = g.sR[11].n
        cstg_rd = cw_st

        for grp in range(9):
            shared = grp == 8
            if not shared:
                e0 = 2 * grp
                if grp % 2 == 0:
                    sy_ = sy(g)
                    sy_.wait_ge(g.sR[11].h, cw_st)
                    sy_.wait_ge(g.sPE.h, cwq_pe)
                    g.sR[10].inc(sy_.dma_start(out=g.smb[0:4, 2560:3072],
                                               in_=g.DW["cw"][4 * (grp // 2):4 * (grp // 2) + 4, tsl]), 16)
                for jj in range(2):
                    j = (e0 + jj) % 4
                    pe_.wait_ge(g.sR[10].h, g.sR[10].n)
                    pe_.wait_ge(g.sDV.h, g.ps_bc_rd_dv)
                    pe_.wait_ge(g.sDV.h, cbc_rd)
                    ins = pe_.matmul(g.ps_bc[:, 512:1024], g.sel[:, 128 * j:128 * (j + 1)],
                                     g.smb[0:4, 2560:3072], start=True, stop=True)
                    g.sPE.inc(ins)
                    cwq_pe = g.sPE.n
                    dv_.wait_ge(g.sPE.h, g.sPE.n)
                    dv_.wait_ge(g.sDV.h, cbc_rd)
                    g.sDV.inc(dv_.tensor_copy(cbc[jj], g.ps_bc[:, 512:1024]))
                    g.ps_bc_rd_dv = g.sDV.n
            cbc_dv = g.sDV.n
            for hu in range(4):
                if shared:
                    j = hu
                    wgsrc = g.DW["sg"][:, :, 256 * j:256 * (j + 1)]
                    wusrc = g.DW["su"][:, :, 256 * j:256 * (j + 1)]
                    wdsrc = g.DW["sd"][:, 2 * j:2 * (j + 1), :]
                else:
                    e = e0 + hu // 2
                    half = hu % 2
                    wgsrc = g.DW["ewg"][e][:, :, 256 * half:256 * (half + 1)]
                    wusrc = g.DW["ewu"][e][:, :, 256 * half:256 * (half + 1)]
                    wdsrc = g.DW["ewd"][e][:, 2 * half:2 * (half + 1), :]
                s0, s1, s2 = (2 * hu) % 4, (2 * hu + 1) % 4, 4 + hu
                sy_ = sy(g)
                sy_.wait_ge(g.sPE.h, slot_pe[s0])
                g.sR[s0].inc(sy_.dma_start(out=rs_[s0].rearrange("p (k o) -> p k o", o=256), in_=wgsrc), 16)
                w0 = g.sR[s0].n
                sy_.wait_ge(g.sPE.h, slot_pe[s1])
                g.sR[s1].inc(sy_.dma_start(out=rs_[s1].rearrange("p (k o) -> p k o", o=256), in_=wusrc), 16)
                w1 = g.sR[s1].n
                sy_.wait_ge(g.sPE.h, slot_pe[s2])
                g.sR[s2].inc(sy_.dma_start(out=rs_[s2].rearrange("p (k o) -> p k o", o=1024), in_=wdsrc), 16)
                pe_.wait_ge(g.sR[s0].h, w0)
                pe_.wait_ge(g.sDV.h, xnm)
                for po2 in range(2):
                    pe_.wait_ge(g.sAC.h, gate_rd[po2])
                    for k in range(NCH):
                        ins = pe_.matmul(pp[po2], rs_[s0][:, 256 * k + 128 * po2: 256 * k + 128 * (po2 + 1)],
                                         xn[k], start=(k == 0), stop=(k == NCH - 1))
                    g.sPE.inc(ins)
                    ac_.wait_ge(g.sPE.h, g.sPE.n)
                    ac_.wait_ge(g.sDV.h, sg_rd[po2])
                    g.sAC.inc(ac_.activation(g.sg_t[po2][:], pp[po2], AF.Silu))
                    gate_rd[po2] = g.sAC.n
                slot_pe[s0] = g.sPE.n
                pe_.wait_ge(g.sR[s1].h, w1)
                for po2 in range(2):
                    pe_.wait_ge(g.sDV.h, up_rd[po2])
                    for k in range(NCH):
                        ins = pe_.matmul(pp[2 + po2], rs_[s1][:, 256 * k + 128 * po2: 256 * k + 128 * (po2 + 1)],
                                         xn[k], start=(k == 0), stop=(k == NCH - 1))
                    g.sPE.inc(ins)
                    dv_.wait_ge(g.sPE.h, g.sPE.n)
                    dv_.wait_ge(g.sAC.h, gate_rd[po2])
                    dv_.wait_ge(g.sPE.h, h_rd)
                    g.sDV.inc(dv_.tensor_tensor(hsl(hu, po2), g.sg_t[po2][:], pp[2 + po2], ALU.mult))
                    up_rd[po2] = sg_rd[po2] = g.sDV.n
                slot_pe[s1] = g.sPE.n
                if not shared:
                    dv_.wait_ge(g.sDV.h, g.sDV.n)
                    for po2 in range(2):
                        ins = dv_.tensor_tensor(hsl(hu, po2), hsl(hu, po2), cbc[hu // 2], ALU.mult)
                    g.sDV.inc(ins)
            hm_dv = hm_pl = g.sDV.n
            cbc_rd = g.sDV.n
            for po in range(8):
                psd = pp[4 + po % 2]
                pe_.wait_ge(g.sDV.h, hm_dv)
                sm, vv = down_ev[po % 2]
                pe_.wait_ge(sm.h, vv)
                if po % 2 == 1:
                    pe_.wait_ge(g.sDV.h, pp5_rd)
                for hu in range(4):
                    pe_.wait_ge(g.sR[4 + hu].h, g.sR[4 + hu].n)
                    for k2 in range(2):
                        ins = pe_.matmul(psd, rs_[4 + hu][:, 1024 * k2 + 128 * po: 1024 * k2 + 128 * (po + 1)],
                                         hsl(hu, k2), start=(hu == 0 and k2 == 0), stop=(hu == 3 and k2 == 1))
                g.sPE.inc(ins)
                if po >= 6:
                    for hu in range(4):
                        slot_pe[4 + hu] = g.sPE.n
                if grp == 0:
                    ac_.wait_ge(g.sPE.h, g.sPE.n)
                    ac_.wait_ge(g.sDV.h, acc_rd)
                    g.sAC.inc(ac_.activation(g.accb[:, po, :], psd, AF.Copy))
                    down_ev[po % 2] = (g.sAC, g.sAC.n)
                else:
                    dv_.wait_ge(g.sPE.h, g.sPE.n)
                    dv_.wait_ge(g.sAC.h, g.sAC.n if grp == 1 else 0)
                    g.sDV.inc(dv_.tensor_tensor(g.accb[:, po, :], g.accb[:, po, :], psd, ALU.add))
                    down_ev[po % 2] = (g.sDV, g.sDV.n)
            h_rd = g.sPE.n

        dv_.wait_ge(g.sDV.h, g.sDV.n)
        sm, vv = down_ev[0]
        dv_.wait_ge(sm.h, vv)
        sm, vv = down_ev[1]
        dv_.wait_ge(sm.h, vv)
        accin = [g.accb[:, c, :] for c in range(NCH)]
        emit_ln(g, accin, None, TC, tmp_bf=tmp_bf, residual=xin)
        acc_rd = g.sDV.n
        xn_rd = g.sPE.n


def make_consts():
    ones = np.ones((128, 128), dtype=ml_dtypes.bfloat16)
    ident = np.eye(128, dtype=ml_dtypes.bfloat16)
    cmask = np.triu(np.full((128, 128), -1e30, dtype=np.float32), k=1)
    sel = np.zeros((4, 512), dtype=ml_dtypes.bfloat16)
    for j in range(4):
        sel[j, 128 * j:128 * (j + 1)] = 1.0
    return ones, ident, cmask, sel


_WMAP = {
    "l0a_qkv": "L0_attn_qkv_weight", "l0a_proj": "L0_attn_proj_weight", "l0a_bias": "L0_attn_proj_bias",
    "l0f_wr": "L0_ffn_Wr", "l0f_wg": "L0_ffn_Wg", "l0f_wu": "L0_ffn_Wu", "l0f_wd": "L0_ffn_Wd",
    "l0f_sg": "L0_ffn_Sg", "l0f_su": "L0_ffn_Su", "l0f_sd": "L0_ffn_Sd",
    "l1a_wq": "L1_attn_Wq", "l1a_wk": "L1_attn_Wk", "l1a_wv": "L1_attn_Wv",
    "l1a_bq": "L1_attn_bq", "l1a_bk": "L1_attn_bk", "l1a_bv": "L1_attn_bv",
    "l1f_wg": "L1_ffn_w_gate", "l1f_wu": "L1_ffn_w_up", "l1f_wd": "L1_ffn_w_down",
}


def kernel(**inputs):
    x = np.asarray(inputs["L0_attn_x"])
    xt = np.ascontiguousarray(x.transpose(0, 2, 1))
    ones, ident, cmask, sel = make_consts()
    nc = build()
    in_maps = []
    for c in range(8):
        m = {"x_in": np.ascontiguousarray(xt[2 * c:2 * c + 2]),
             "c_ones": ones, "c_id": ident, "c_mask": cmask, "c_sel": sel}
        for k, v in _WMAP.items():
            m[k] = np.ascontiguousarray(np.asarray(inputs[v]))
        in_maps.append(m)
    res = run_bass_kernel_spmd(nc, in_maps, core_ids=list(range(8)))
    import kernel as _K
    _K.LAST_DBG = (np.asarray(res.results[0]["dbg_bf"]), np.asarray(res.results[0]["dbg_f"]))
    out = np.concatenate([r["x_out"] for r in res.results], axis=0)
    return np.ascontiguousarray(out.transpose(0, 2, 1)).astype(np.float32)
